# version 22
# baseline (speedup 1.0000x reference)
import numpy as np
from contextlib import ExitStack

import concourse.bass as bass
import concourse.mybir as mybir
from concourse.bass_utils import run_bass_kernel_spmd

F32 = mybir.dt.float32
BF16 = mybir.dt.bfloat16
I32 = mybir.dt.int32
ALU = mybir.AluOpType
ACT = mybir.ActivationFunctionType

T = 4096
D = 1024
DH = 8
NST = 16
TC = 512
NCH = T // TC
R = 32

ENGINES = ("sync", "scalar", "vector", "tensor", "gpsimd")
SEMS = ("xin", "wcol", "wst", "sout", "T", "S", "V", "G",
        "wsA", "wsB", "wq0", "wq1", "dqA", "dqB", "wgp", "wiA", "wiB")

OF_CONV = 0
OF_CB = 32
OF_D = 40
OF_BIAS = 48
OF_G = 96
OF_B = 120
OF_ONE = 144
OF_EPS = 145
OF_LOGA = 152
COLS_W = 280


class Plan:
    def __init__(self):
        self.q = {e: [] for e in ENGINES}
        self.cnt = {s: 0 for s in SEMS}
        self.maxw = {e: {s: 0 for s in SEMS} for e in ENGINES}

    def wait(self, e, sem, val):
        if val <= 0 or val <= self.maxw[e][sem]:
            return
        self.maxw[e][sem] = val
        self.q[e].append(("w", sem, val))

    def emit(self, e, fn, incs=()):
        self.q[e].append(("i", fn, incs))
        for s, a in incs:
            self.cnt[s] += a


def build_program(nch=NCH):
    nc = bass.Bass("TRN2", target_bir_lowering=False, detect_race_conditions=False)

    x_in = nc.dram_tensor("x_fm", [D, T], F32, kind="ExternalInput").ap()
    w_in = nc.dram_tensor("w_in", [D, 2 * D], F32, kind="ExternalInput").ap()
    w_xp = nc.dram_tensor("w_xp", [D, 33], F32, kind="ExternalInput").ap()
    w_out = nc.dram_tensor("w_out", [D, D], F32, kind="ExternalInput").ap()
    cols_in = nc.dram_tensor("cols", [128, COLS_W], F32, kind="ExternalInput").ap()
    sel_in = nc.dram_tensor("selB", [128, 16 * 128], BF16, kind="ExternalInput").ap()
    zf32_in = nc.dram_tensor("zf32", [128, 1024], F32, kind="ExternalInput").ap()
    zbf_in = nc.dram_tensor("zbf", [128, 1024], BF16, kind="ExternalInput").ap()
    zone_in = nc.dram_tensor("zone_in", [128, 128], F32, kind="ExternalInput").ap()
    onesb_in = nc.dram_tensor("onesb_in", [128, 128], BF16, kind="ExternalInput").ap()
    qT_in = nc.dram_tensor("qT", [6, D, D], I32, kind="ExternalInput").ap()
    sT_in = nc.dram_tensor("sT", [6, D, D], F32, kind="ExternalInput").ap()
    laT_in = nc.dram_tensor("laT", [6, D, R], F32, kind="ExternalInput").ap()
    lbT_in = nc.dram_tensor("lbT", [6, R, D], F32, kind="ExternalInput").ap()
    out_fm = nc.dram_tensor("out_fm", [D, T], F32, kind="ExternalOutput").ap()
    wffn_sc = nc.dram_tensor("wffn_sc", [6, 2, DH, 128, TC], BF16, kind="Internal").ap()
    wout_sc = nc.dram_tensor("wout_sc", [DH, 128, D], BF16, kind="Internal").ap()
    win_sc = nc.dram_tensor("win_sc", [8, DH, 128, 256], BF16, kind="Internal").ap()
    lbT_sc = nc.dram_tensor("lbT_sc", [6, R, D], BF16, kind="Internal").ap()

    st = ExitStack()
    en = st.enter_context

    def sbuf(name, shape, dt):
        return en(nc.sbuf_tensor(name, shape, dt))

    cols = sbuf("colsb", [128, COLS_W], F32)
    lam = sbuf("lam", [128, 128], F32)
    selB = sbuf("selBb", [128, 16 * 128], BF16)
    zone = sbuf("zone", [128, 128], F32)
    onesb = sbuf("onesb", [128, 128], BF16)
    wx_bf = sbuf("wx_bf", [128, DH, 33], BF16)
    la_bf = sbuf("la_bf", [128, 6 * DH * R], BF16)
    lb_bf = sbuf("lb_bf", [128, 1, D], BF16)
    winr = sbuf("winr", [128, 4, TC], BF16)
    wk = sbuf("wk", [128, 8, TC], BF16)
    qstg = sbuf("qstg", [128, 1, TC], I32)
    sstg = sbuf("sstg", [128, 1, TC], F32)
    gq = sbuf("gq", [128, TC], F32)
    dqo = sbuf("dqo", [128, 1, TC], BF16)

    x8 = sbuf("x8", [128, 2 * DH, TC], F32)
    xmix = sbuf("xmix", [128, DH, TC], BF16)
    xhat = sbuf("xhat", [128, DH, TC], BF16)
    cast2 = sbuf("cast2", [128, 2, TC], BF16)
    trot = sbuf("trot", [128, 1, TC], F32)
    xm = sbuf("xm", [128, 2, 3 + TC], F32)
    hist = sbuf("hist", [128, 24], F32)
    vrot = sbuf("vrot", [128, 2, TC], F32)
    gz = sbuf("gz", [128, DH, TC], BF16)
    u8d = sbuf("u8d", [128, 2 * DH, TC], BF16)
    ubf = sbuf("ubf", [128, 2, TC], BF16)
    xb2 = sbuf("xb2", [128, 2, TC], BF16)
    xc2 = sbuf("xc2", [128, 2, TC], BF16)
    dbsb = sbuf("dbsb", [128, TC], F32)
    bbn = sbuf("bbn", [128, 2, TC], F32)
    arot = sbuf("arot", [128, 3, TC], F32)
    shim1 = sbuf("shim1", [128, 160], F32)
    brot = sbuf("brot", [128, 3, TC], F32)
    hrot = sbuf("hrot", [128, 3, TC], F32)
    cbn = sbuf("cbn", [128, 4, TC], F32)
    carry = sbuf("carry", [128, 128], F32)
    yacc = sbuf("yacc", [128, DH, TC], F32)
    ygbf = sbuf("ygbf", [128, DH, TC], BF16)
    rbsb = sbuf("rbsb", [128, TC], F32)
    rbB = sbuf("rbB", [128, TC], F32)
    y18 = sbuf("y18", [128, DH, TC], BF16)
    gbf = sbuf("gbf", [128, DH, TC], BF16)
    xabf = sbuf("xabf", [128, TC], BF16)
    mub = sbuf("mub", [128, TC], F32)
    s2sb = sbuf("s2sb", [128, TC], F32)
    gvro = sbuf("gvro", [128, 2, TC], F32)

    pb = [en(nc.psum_tensor(f"pb{i}", [128, TC], F32)) for i in range(8)]

    sem = {s: en(nc.semaphore(s)) for s in SEMS}

    p = Plan()
    C = p.cnt

    def col(off):
        return cols[:, off:off + 1]

    x_r = x_in.rearrange("(dh p) t -> p dh t", p=128)
    o_r = out_fm.rearrange("(dh p) t -> p dh t", p=128)

    for dst, srcap in [
        (lambda: cols[:], cols_in), (lambda: selB[:], sel_in),
        (lambda: zone[:], zone_in), (lambda: onesb[:], onesb_in),
        (lambda: carry[:], zf32_in[:, 0:128]), (lambda: hist[:], zf32_in[:, 0:24]),
        (lambda: xb2[:, 0, :], zbf_in[:, 0:TC]),
        (lambda: xb2[:, 1, :], zbf_in[:, 0:TC]), (lambda: xc2[:, 0, :], zbf_in[:, 0:TC]),
        (lambda: xc2[:, 1, :], zbf_in[:, 0:TC]), (lambda: xabf[:], zbf_in[:, 0:TC]),
        (lambda: lb_bf[:, 0, :], zbf_in),
    ]:
        p.emit("sync", lambda s_, d=dst, sa=srcap: s_.dma_start(out=d(), in_=sa), [("wcol", 16)])
    p.emit("sync", lambda s_: s_.dma_start(out=x8[:, 0:DH, :], in_=x_r[:, :, 0:TC]), [("xin", 16)])
    WCOL = C["wcol"]
    p.wait("scalar", "wcol", WCOL)
    p.emit("scalar", lambda s_: s_.activation(out=lam[:], in_=cols[:, OF_LOGA:OF_LOGA + 128], func=ACT.Exp), [("S", 1)])
    p.wait("vector", "S", C["S"])
    p.emit("vector", lambda v: v.tensor_scalar(out=lam[:], in0=lam[:], scalar1=-1.0, scalar2=None, op0=ALU.mult), [("V", 1)])

    w_in_r = w_in.rearrange("(dh p) m -> dh p m", p=128)
    w_xp_r = w_xp.rearrange("(dh p) m -> p dh m", p=128)
    la_r = laT_in.rearrange("l (kt p) r -> p l kt r", p=128)
    la_bf_v = la_bf[:].rearrange("p (l kt r) -> p l kt r", l=6, kt=DH)
    w_out_r = w_out.rearrange("(dh p) m -> dh p m", p=128)
    for q in range(8):
        p.emit("gpsimd", lambda g, qq=q: g.dma_start(
            out=win_sc[qq], in_=w_in_r[:, :, qq * 256:(qq + 1) * 256]), [("wgp", 16)])
    p.emit("gpsimd", lambda g: g.dma_start(out=wx_bf[:], in_=w_xp_r), [("wgp", 16)])
    p.emit("gpsimd", lambda g: g.dma_start(out=la_bf_v, in_=la_r), [("wgp", 16)])
    p.emit("gpsimd", lambda g: g.dma_start(out=wout_sc[:], in_=w_out_r), [("wgp", 16)])
    p.emit("gpsimd", lambda g: g.dma_start(out=lbT_sc[:], in_=lbT_in), [("wgp", 16)])
    WGP_ALL = C["wgp"]

    dq_pending = []
    dq_lin_done = []
    dq_state = {"npc": 0, "g": {}, "st": []}

    def mk_dq(l, kt, hf):
        def run():
            npc = dq_state["npc"]
            dqs = "dqA" if npc % 2 == 0 else "dqB"
            if npc >= 1:
                p.wait("sync", "G", dq_state["g"][npc - 1])
            p.emit("sync", lambda s_: s_.dma_start(
                out=qstg[:, 0, :], in_=qT_in[l, kt * 128:(kt + 1) * 128, hf * TC:(hf + 1) * TC]), [(dqs, 16)])
            p.emit("sync", lambda s_: s_.dma_start(
                out=sstg[:, 0, :], in_=sT_in[l, kt * 128:(kt + 1) * 128, hf * TC:(hf + 1) * TC]), [(dqs, 16)])
            p.wait("gpsimd", dqs, C[dqs])
            if npc >= 1:
                p.wait("gpsimd", dq_state["st"][npc - 1][0], dq_state["st"][npc - 1][1])
            p.emit("gpsimd", lambda g: g.tensor_scalar(
                out=gq[:], in0=qstg[:, 0, :], scalar1=1.0 / 7.5, scalar2=-1.0, op0=ALU.mult, op1=ALU.add))
            p.emit("gpsimd", lambda g: g.tensor_tensor(
                out=dqo[:, 0, :], in0=gq[:], in1=sstg[:, 0, :], op=ALU.mult), [("G", 1)])
            dq_state["g"][npc] = C["G"]
            p.wait("sync", "G", C["G"])
            wqs = "wq0" if npc % 2 == 0 else "wq1"
            p.emit("sync", lambda s_: s_.dma_start(out=wffn_sc[l, hf, kt], in_=dqo[:, 0, :]), [(wqs, 16)])
            dq_state["st"].append((wqs, C[wqs]))
            dq_state["npc"] = npc + 1
        return run

    nq0 = nq1 = 0
    for l in range(6):
        for kt in range(DH):
            for hf in range(2):
                dq_pending.append(mk_dq(l, kt, hf))
                if (l * 16 + kt * 2 + hf) % 2 == 0:
                    nq0 += 1
                else:
                    nq1 += 1
        dq_lin_done.append([("wq0", 16 * nq0), ("wq1", 16 * nq1)])

    S_ = {
        "v_done127": 0, "g_done127": 0,
        "tbb15": 0, "tcb15": 0, "sbb15": 0, "scb15": 0,
        "vgate": 0, "vconv7": 0, "v_ln1": 0, "v_ln2": 0, "v_rbstats": 0,
        "t_wout": 0, "t_blast": 0, "g_blast": 0, "t_laststats": 0,
        "t_qstop": 0, "s_conv_last": 0, "s_zdone": 0,
        "bank_free_S": [0] * 8, "bank_free_V": [0] * 8,
        "s_carry_all": 0, "g_b_last": 0, "vbp": 0, "vresid": 0,
        "t_lb": 0, "s_xhat": 0, "sdb": 0, "t_wklast": 0,
    }
    A1D = {}
    pending = []

    def pump(k=3):
        for _ in range(k):
            if pending:
                pending.pop(0)()
            elif dq_pending:
                dq_pending.pop(0)()
        for _ in range(2):
            if dq_pending:
                dq_pending.pop(0)()

    def flush():
        while dq_pending:
            dq_pending.pop(0)()
        while pending:
            pending.pop(0)()

    def rb_from_stats(bank_mu, bank_s2, rb_dst):
        tstats = C["T"]
        p.wait("scalar", "T", tstats)
        p.wait("scalar", "V", S_["v_ln1"])
        p.wait("scalar", "V", S_["v_ln2"])
        p.emit("scalar", lambda s_, b=bank_mu: s_.activation(out=trot[:, 0, :], in_=pb[b][:], func=ACT.Square), [("S", 1)])
        p.wait("vector", "S", C["S"])
        p.wait("vector", "T", tstats)
        p.wait("vector", "S", S_["s_conv_last"])
        p.emit("vector", lambda v, b=bank_s2: v.tensor_tensor(out=vrot[:, 0, :], in0=pb[b][:], in1=trot[:, 0, :], op=ALU.subtract), [("V", 1)])
        S_["v_rbstats"] = C["V"]
        p.wait("scalar", "V", C["V"])
        p.emit("scalar", lambda s_, r=rb_dst: s_.activation(out=r[:], in_=vrot[:, 0, :], func=ACT.Ln, bias=col(OF_EPS)))
        p.emit("scalar", lambda s_, r=rb_dst: s_.activation(out=r[:], in_=r[:], func=ACT.Exp, scale=-0.5), [("S", 1)])
        return C["S"]

    def stats01(src_getter):
        p.wait("scalar", "T", S_["t_blast"])
        s0 = C["S"]
        tb0 = C["T"]
        for kt in range(DH):
            if kt >= 2:
                p.wait("scalar", "T", tb0 + kt - 1)
            p.emit("scalar", lambda s_, k=kt: s_.copy(out=cast2[:, k % 2, :], in_=src_getter(k)))
            p.emit("scalar", lambda s_, k=kt: s_.activation(out=gbf[:, k, :], in_=src_getter(k), func=ACT.Square), [("S", 1)])
            p.wait("tensor", "S", s0 + kt + 1)
            p.emit("tensor", lambda t_, k=kt: t_.matmul(out=pb[0][:], lhsT=onesb[:], rhs=cast2[:, k % 2, :], start=(k == 0), stop=(k == DH - 1)))
            p.emit("tensor", lambda t_, k=kt: t_.matmul(out=pb[1][:], lhsT=onesb[:], rhs=gbf[:, k, :], start=(k == 0), stop=(k == DH - 1)), [("T", 1)])
        S_["t_laststats"] = C["T"]
        return C["T"]

    def make_A1(c1):
        par1 = (c1 % 2) * DH
        A = {"vconv": {}, "ssilu": {}}
        A1D[c1] = A
        th = []

        def xs1(dh):
            return x8[:, par1 + dh, :]

        def a1_stats():
            for e in ("scalar", "tensor"):
                p.wait(e, "xin", 16 * (c1 + 1))
                p.wait(e, "wcol", WCOL)
            p.wait("tensor", "wgp", WGP_ALL)
            p.wait("tensor", "S", S_["bank_free_S"][0])
            p.wait("tensor", "S", S_["bank_free_S"][1])
            p.wait("tensor", "V", S_["v_ln2"])
            p.wait("tensor", "V", S_["v_rbstats"])
            p.wait("scalar", "T", S_["t_laststats"])
            stats01(xs1)
        th.append(a1_stats)

        def a1_rb():
            A["srb"] = rb_from_stats(0, 1, rbsb)
            S_["bank_free_S"][0] = A["srb"]
            S_["bank_free_S"][1] = A["srb"]
        th.append(a1_rb)

        def a1_apply():
            p.wait("vector", "S", A["srb"])
            p.wait("vector", "T", S_["t_qstop"])
            for kt in range(DH):
                p.emit("vector", lambda v, k=kt: v.tensor_tensor(out=trot[:, 0, :], in0=xs1(k), in1=pb[0][:], op=ALU.subtract))
                p.emit("vector", lambda v, k=kt: v.tensor_tensor(out=xmix[:, k, :], in0=trot[:, 0, :], in1=rbsb[:], op=ALU.mult),
                       [("V", 1)] if kt == DH - 1 else ())
            A["v_ln1"] = C["V"]
            S_["v_ln1"] = C["V"]
        th.append(a1_apply)

        txp = {}

        def mk_pass(pp):
            def a1_pass():
                wis = "wiA" if pp % 2 == 0 else "wiB"
                tp0 = C["T"]
                wivals = []
                for d in range(4):
                    p.wait("sync", "T", tp0 + d - 3)
                    p.emit("sync", lambda s_, dd=d: s_.dma_start(
                        out=winr[:, dd % 4, :].rearrange("p (a c) -> p a c", a=2),
                        in_=win_sc[pp, 2 * dd:2 * dd + 2].rearrange("a p c -> p a c")), [(wis, 16)])
                    wivals.append((wis, C[wis]))
                p.wait("tensor", "V", A["v_ln1"])
                for kt in range(DH):
                    d = kt // 2
                    sub = kt % 2
                    p.wait("tensor", wivals[d][0], wivals[d][1])
                    for j in range(2):
                        bank = 2 + j
                        if kt == 0:
                            p.wait("tensor", "S", S_["bank_free_S"][bank])
                            p.wait("tensor", "V", S_["bank_free_V"][bank])
                        p.emit("tensor", lambda t_, k=kt, dd=d, sb=sub, jj=j, b=bank: t_.matmul(
                            out=pb[b][:], lhsT=winr[:, dd % 4, sb * 256 + jj * 128:sb * 256 + (jj + 1) * 128],
                            rhs=xmix[:, k, :], start=(k == 0), stop=(k == DH - 1)),
                            [("T", 1)] if (sub == 1 and j == 1) else ())
                tp_done = C["T"]
                for j in range(2):
                    m = 2 * pp + j
                    bank = 2 + j
                    p.wait("scalar", "T", tp_done)
                    dh = m
                    sl = m % 2
                    p.wait("scalar", "V", A["vconv"][m - 2] if m >= 2 else S_["vconv7"])
                    p.emit("scalar", lambda s_, d2=dh, s3=sl: s_.copy(out=xm[:, s3, 0:3], in_=hist[:, d2 * 3:(d2 + 1) * 3]))
                    p.emit("scalar", lambda s_, d2=dh, s3=sl, b=bank: s_.copy(out=xm[:, s3, 3:3 + TC], in_=pb[b][:]), [("S", 1)])
                    sev = C["S"]
                    S_["bank_free_S"][bank] = sev
                    p.wait("vector", "S", sev)
                    if m >= 2:
                        p.wait("vector", "S", A["ssilu"][m - 2])
                    p.emit("vector", lambda v, d2=dh, s3=sl: v.tensor_scalar(
                        out=vrot[:, d2 % 2, :], in0=xm[:, s3, 0:TC], scalar1=col(OF_CONV + d2 * 4 + 0),
                        scalar2=col(OF_CB + d2), op0=ALU.mult, op1=ALU.add))
                    for k in range(1, 4):
                        p.emit("vector", lambda v, d2=dh, s3=sl, kk=k: v.scalar_tensor_tensor(
                            out=vrot[:, d2 % 2, :], in0=xm[:, s3, kk:kk + TC], scalar=col(OF_CONV + d2 * 4 + kk),
                            in1=vrot[:, d2 % 2, :], op0=ALU.mult, op1=ALU.add), [("V", 1)] if k == 3 else ())
                    A["vconv"][m] = C["V"]
                    if m == 7:
                        S_["vconv7"] = C["V"]
                    p.wait("scalar", "V", A["vconv"][m])
                    p.emit("scalar", lambda s_, d2=dh, s3=sl: s_.copy(out=hist[:, d2 * 3:(d2 + 1) * 3], in_=xm[:, s3, TC:TC + 3]))
                    p.emit("scalar", lambda s_, d2=dh: s_.activation(out=u8d[:, par1 + d2, :], in_=vrot[:, d2 % 2, :], func=ACT.Silu), [("S", 1)])
                    A["ssilu"][m] = C["S"]
                    S_["s_conv_last"] = C["S"]
                    if m >= 2:
                        p.wait("scalar", "T", txp[m - 2])
                    p.emit("scalar", lambda s_, d2=dh: s_.copy(out=ubf[:, d2 % 2, :], in_=u8d[:, par1 + d2, :]), [("S", 1)])
                    scst = C["S"]
                    p.wait("tensor", "S", scst)
                    if m == 0:
                        p.wait("tensor", "V", S_["vbp"])
                    p.emit("tensor", lambda t_, k=dh: t_.matmul(out=pb[4][0:16, :], lhsT=wx_bf[:, k, 0:16], rhs=ubf[:, k % 2, :],
                                                                start=(k == 0), stop=(k == DH - 1), skip_group_check=True))
                    p.emit("tensor", lambda t_, k=dh: t_.matmul(out=pb[4][32:48, :], lhsT=wx_bf[:, k, 16:32], rhs=ubf[:, k % 2, :],
                                                                start=(k == 0), stop=(k == DH - 1), skip_group_check=True))
                    p.emit("tensor", lambda t_, k=dh: t_.matmul(out=pb[4][64:65, :], lhsT=wx_bf[:, k, 32:33], rhs=ubf[:, k % 2, :],
                                                                start=(k == 0), stop=(k == DH - 1), skip_group_check=True), [("T", 1)])
                    txp[m] = C["T"]
                    if m == 7:
                        A["t_xproj"] = C["T"]
                S_["t_qstop"] = C["T"]
            return a1_pass

        for pp in range(4):
            th.append(mk_pass(pp))

        def a1_soft():
            p.wait("scalar", "T", A["t_xproj"])
            p.wait("scalar", "T", S_["tcb15"])
            p.wait("scalar", "V", A["v_ln1"])
            p.emit("scalar", lambda s_: s_.copy(out=xc2[0:16, c1 % 2, :], in_=pb[4][32:48, :]))
            p.emit("scalar", lambda s_: s_.activation(out=rbsb[0:1, 0:TC], in_=pb[4][64:65, :], func=ACT.Abs))
            p.emit("scalar", lambda s_: s_.activation(out=rbsb[0:1, 0:TC], in_=rbsb[0:1, 0:TC], func=ACT.Exp, scale=-1.0))
            p.emit("scalar", lambda s_: s_.activation(out=rbsb[0:1, 0:TC], in_=rbsb[0:1, 0:TC], func=ACT.Ln, bias=cols[0:1, OF_ONE:OF_ONE + 1]))
            p.emit("scalar", lambda s_: s_.activation(out=trot[0:1, 0, 0:TC], in_=pb[4][64:65, :], func=ACT.Relu), [("S", 1)])
            sxp = C["S"]
            p.wait("vector", "S", sxp)
            p.emit("vector", lambda v: v.tensor_tensor(out=rbsb[0:1, 0:TC], in0=rbsb[0:1, 0:TC], in1=trot[0:1, 0, 0:TC], op=ALU.add), [("V", 1)])
            A["v_dsb"] = C["V"]
        th.append(a1_soft)
        return th

    def _mk_z(c1):
        ths = []

        def mk_zpass(pp):
            def zpass():
                wis = "wiA" if pp % 2 == 0 else "wiB"
                tp0 = C["T"]
                wivals = []
                for d in range(4):
                    p.wait("sync", "T", tp0 + d - 3)
                    p.emit("sync", lambda s_, dd=d: s_.dma_start(
                        out=winr[:, dd % 4, :].rearrange("p (a c) -> p a c", a=2),
                        in_=win_sc[pp, 2 * dd:2 * dd + 2].rearrange("a p c -> p a c")), [(wis, 16)])
                    wivals.append((wis, C[wis]))
                p.wait("tensor", "V", A1D[c1]["v_ln1"])
                for kt in range(DH):
                    d = kt // 2
                    sub = kt % 2
                    p.wait("tensor", wivals[d][0], wivals[d][1])
                    for j in range(2):
                        bank = 2 + j
                        if kt == 0:
                            p.wait("tensor", "S", S_["bank_free_S"][bank])
                            p.wait("tensor", "V", S_["bank_free_V"][bank])
                        p.emit("tensor", lambda t_, k=kt, dd=d, sb=sub, jj=j, b=bank: t_.matmul(
                            out=pb[b][:], lhsT=winr[:, dd % 4, sb * 256 + jj * 128:sb * 256 + (jj + 1) * 128],
                            rhs=xmix[:, k, :], start=(k == 0), stop=(k == DH - 1)),
                            [("T", 1)] if (sub == 1 and j == 1) else ())
                tp_done = C["T"]
                for j in range(2):
                    m = 2 * pp + j
                    dh = m - 8
                    bank = 2 + j
                    p.wait("scalar", "T", tp_done)
                    p.wait("scalar", "V", S_["vgate"])
                    p.emit("scalar", lambda s_, d2=dh, b=bank: s_.activation(out=gz[:, d2, :], in_=pb[b][:], func=ACT.Silu), [("S", 1)])
                    S_["bank_free_S"][bank] = C["S"]
                    S_["s_zdone"] = C["S"]
            return zpass
        for pp in range(4, 8):
            ths.append(mk_zpass(pp))
        return ths

    def emit_scan(c):
        par = (c % 2) * DH

        p.wait("vector", "S", A1D[c]["ssilu"][7])
        for dh in range(DH):
            p.emit("vector", lambda v, d=dh: v.tensor_scalar(
                out=yacc[:, d, :], in0=u8d[:, par + d, :], scalar1=col(OF_D + d), scalar2=None, op0=ALU.mult),
                [("V", 1)] if dh == DH - 1 else ())

        v_done = {}
        g_done = {}
        sa_done = {}
        scarry = {}
        sbb = {}
        scb = {}
        tbb = {}
        tcb = {}
        prev_vd127 = S_["v_done127"]
        prev_sbb15 = S_["sbb15"]
        prev_scb15 = S_["scb15"]
        prev_gb = S_["g_b_last"]
        def emit_bbcb_mm(n):
            p.wait("tensor", "S", sbb[n - 1] if n >= 1 else prev_sbb15)
            p.wait("tensor", "V", S_["vbp"])
            p.emit("tensor", lambda t_, nn=n: t_.matmul(out=pb[5][:], lhsT=selB[:, nn * 128:(nn + 1) * 128], rhs=xb2[:, c % 2, :], start=True, stop=True), [("T", 1)])
            tbb[n] = C["T"]
            p.wait("tensor", "S", scb[n - 2] if n >= 2 else prev_scb15)
            if n < 2:
                p.wait("tensor", "S", S_["sdb"])
            p.emit("tensor", lambda t_, nn=n: t_.matmul(out=pb[6 + nn % 2][:], lhsT=selB[:, nn * 128:(nn + 1) * 128], rhs=xc2[:, c % 2, :], start=True, stop=True), [("T", 1)])
            tcb[n] = C["T"]

        emit_bbcb_mm(0)
        for n in range(NST):
            p.wait("scalar", "T", tbb[n])
            p.wait("scalar", "G", g_done[(n - 1) * 8 - 1] if n >= 2 else prev_gb)
            p.emit("scalar", lambda s_, nn=n: s_.copy(out=bbn[:, nn % 2, :], in_=pb[5][:]), [("S", 1)])
            sbb[n] = C["S"]
            p.wait("scalar", "T", tcb[n])
            p.wait("scalar", "V", v_done[(n - 3) * 8 - 1] if n >= 4 else prev_vd127)
            p.emit("scalar", lambda s_, nn=n: s_.copy(out=cbn[:, nn % 4, :], in_=pb[6 + nn % 2][:]), [("S", 1)])
            scb[n] = C["S"]
            if n + 1 < NST:
                emit_bbcb_mm(n + 1)
            for dh in range(DH):
                i = n * 8 + dh
                if i >= 3:
                    p.wait("scalar", "V", v_done[i - 3])
                    p.emit("scalar", lambda s_, ii=i - 3: s_.copy(out=carry[:, ii:ii + 1], in_=hrot[:, ii % 3, TC - 1:TC]))
                p.emit("scalar", lambda s_, nn=n, d=dh, sl=i % 3: s_.activation(
                    out=arot[:, sl, :], in_=dbsb[:], func=ACT.Exp, scale=lam[:, d * 16 + nn:d * 16 + nn + 1]), [("S", 1)])
                sa_done[i] = C["S"]
                p.wait("gpsimd", "S", sbb[n])
                p.wait("gpsimd", "V", v_done[i - 2] if i >= 2 else prev_vd127)
                p.emit("gpsimd", lambda g, nn=n, d=dh, sl=i % 3: g.tensor_tensor(
                    out=brot[:, sl, :], in0=u8d[:, par + d, :], in1=bbn[:, nn % 2, :], op=ALU.mult), [("G", 1)])
                g_done[i] = C["G"]
                p.wait("vector", "S", sa_done[i])
                p.wait("vector", "G", g_done[i])
                if i < 3:
                    p.wait("vector", "S", S_["s_carry_all"])
                p.emit("vector", lambda v, sl=i % 3, ii=i: v.tensor_tensor_scan(
                    out=hrot[:, sl, :], data0=arot[:, sl, :], data1=brot[:, sl, :],
                    initial=carry[:, ii:ii + 1], op0=ALU.mult, op1=ALU.add))
                if i >= 1:
                    p.emit("vector", lambda v, slp=(i - 1) % 3, dp=(i - 1) % 8: v.tensor_tensor(
                        out=yacc[:, dp, :], in0=brot[:, slp, :], in1=yacc[:, dp, :], op=ALU.add))
                p.wait("vector", "S", scb[n])
                p.emit("vector", lambda v, sl=i % 3, nn=n: v.tensor_tensor(
                    out=brot[:, sl, :], in0=hrot[:, sl, :], in1=cbn[:, nn % 4, :], op=ALU.mult), [("V", 1)])
                v_done[i] = C["V"]
            pump(2)
        p.emit("vector", lambda v: v.tensor_tensor(
            out=yacc[:, 7, :], in0=brot[:, 127 % 3, :], in1=yacc[:, 7, :], op=ALU.add), [("V", 1)])
        v_add_last = C["V"]
        for ii in range(NST * 8 - 3, NST * 8):
            p.wait("scalar", "V", v_done[ii])
            p.emit("scalar", lambda s_, i2=ii: s_.copy(out=carry[:, i2:i2 + 1], in_=hrot[:, i2 % 3, TC - 1:TC]), [("S", 1)])
        S_["s_carry_all"] = C["S"]
        S_["v_done127"] = v_add_last
        S_["g_done127"] = g_done[NST * 8 - 1]
        S_["g_b_last"] = g_done[NST * 8 - 1]
        S_["tbb15"] = tbb[NST - 1]
        S_["tcb15"] = tcb[NST - 1]
        S_["sbb15"] = sbb[NST - 1]
        S_["scb15"] = scb[NST - 1]
        S_["bank_free_S"][5] = sbb[NST - 1]
        S_["bank_free_S"][6] = max(S_["bank_free_S"][6], scb[NST - 2])
        S_["bank_free_S"][7] = max(S_["bank_free_S"][7], scb[NST - 1])

        flush()

        p.wait("vector", "S", S_["s_zdone"])
        for dh in range(DH):
            p.emit("vector", lambda v, d=dh: v.tensor_tensor(out=ygbf[:, d, :], in0=yacc[:, d, :], in1=gz[:, d, :], op=ALU.mult),
                   [("V", 1)] if dh == DH - 1 else ())
        S_["vgate"] = C["V"]

    def emit_wout(cm):
        parm = (cm % 2) * DH
        p.wait("sync", "wgp", WGP_ALL)
        p.wait("sync", "T", S_["t_wklast"])
        tH0 = C["T"]
        wkt = []
        for h in range(2):
            for kt in range(DH):
                idx = h * 8 + kt
                if h == 1:
                    p.wait("sync", "T", tH0 + 8)
                wqs = "wsA" if idx % 2 == 0 else "wsB"
                p.emit("sync", lambda s_, k=kt, hh=h: s_.dma_start(out=wk[:, k, 0:TC], in_=wout_sc[k][:, hh * TC:(hh + 1) * TC]), [(wqs, 16)])
                wkt.append((wqs, C[wqs]))
        p.wait("tensor", "V", S_["vgate"])
        vres = {}
        for h in range(2):
            for b4 in range(4):
                p.wait("tensor", "S", S_["bank_free_S"][b4])
                p.wait("tensor", "V", S_["bank_free_V"][b4])
            if h == 1:
                p.wait("tensor", "V", vres[0])
            for kt in range(DH):
                idx = h * 8 + kt
                p.wait("tensor", wkt[idx][0], wkt[idx][1])
                for j in range(4):
                    p.emit("tensor", lambda t_, k=kt, hh=h, jj=j: t_.matmul(
                        out=pb[jj][:], lhsT=wk[:, k, jj * 128:(jj + 1) * 128], rhs=ygbf[:, k, :],
                        start=(k == 0), stop=(k == DH - 1)),
                        [("T", 1)] if j == 3 else ())
            tHd = C["T"]
            p.wait("vector", "T", tHd)
            for j in range(4):
                m = h * 4 + j
                p.emit("vector", lambda v, mm=m, jj=j: v.tensor_tensor(out=x8[:, parm + mm, :], in0=x8[:, parm + mm, :], in1=pb[jj][:], op=ALU.add),
                       [("V", 1)] if j == 3 else ())
            vres[h] = C["V"]
        S_["t_wout"] = C["T"]
        S_["t_wklast"] = C["T"]
        S_["vresid"] = vres[1]
        for b4 in range(4):
            S_["bank_free_V"][b4] = vres[1]

    def emit_delta(c):
        A = A1D[c]
        p.wait("tensor", "V", A["v_dsb"])
        p.wait("tensor", "S", S_["scb15"])
        p.wait("tensor", "T", A["t_xproj"])
        p.emit("tensor", lambda t_: t_.matmul(out=pb[6][:], lhsT=zone[:], rhs=rbsb[:, 0:TC], start=True, stop=True), [("T", 1)])
        p.wait("scalar", "T", C["T"])
        p.emit("scalar", lambda s_: s_.copy(out=dbsb[:], in_=pb[6][:]), [("S", 1)])
        S_["sdb"] = C["S"]
        p.wait("vector", "S", S_["sdb"])
        p.wait("vector", "T", S_["tbb15"])
        p.emit("vector", lambda v, cc=c: v.tensor_tensor(out=xb2[0:16, cc % 2, :], in0=pb[4][0:16, :], in1=dbsb[0:16, :], op=ALU.mult), [("V", 1)])
        S_["vbp"] = C["V"]

    def make_B(c):
        par = (c % 2) * DH

        def xs(dh):
            return x8[:, par + dh, :]

        thunks = []

        def th(fn):
            thunks.append(fn)

        state = {}

        def ln2_stats():
            p.wait("tensor", "V", S_["vresid"])
            p.wait("tensor", "S", S_["bank_free_S"][0])
            p.wait("tensor", "S", S_["bank_free_S"][1])
            p.wait("tensor", "G", S_["g_blast"])
            p.wait("tensor", "T", S_["t_blast"])
            p.wait("scalar", "T", S_["t_laststats"])
            p.wait("scalar", "V", S_["vresid"])
            p.wait("scalar", "G", S_["g_blast"])
            stats01(xs)
        th(ln2_stats)

        def ln2_rb():
            state["srb2"] = rb_from_stats(0, 1, rbB)
            S_["bank_free_S"][0] = state["srb2"]
            S_["bank_free_S"][1] = state["srb2"]
        th(ln2_rb)

        def ln2_apply():
            p.wait("vector", "S", state["srb2"])
            scast = 0
            for kt in range(DH):
                if kt > 0:
                    p.wait("vector", "S", scast)
                p.emit("vector", lambda v, k=kt: v.tensor_tensor(out=trot[:, 0, :], in0=xs(k), in1=pb[0][:], op=ALU.subtract))
                p.emit("vector", lambda v, k=kt: v.tensor_tensor(out=trot[:, 0, :], in0=trot[:, 0, :], in1=rbB[:], op=ALU.mult))
                p.emit("vector", lambda v, k=kt: v.tensor_tensor(out=xs(k), in0=xs(k), in1=trot[:, 0, :], op=ALU.add), [("V", 1)])
                p.wait("scalar", "V", C["V"])
                p.emit("scalar", lambda s_, k=kt: s_.copy(out=xhat[:, k, :], in_=trot[:, 0, :]), [("S", 1)])
                scast = C["S"]
            state["v_ln2"] = C["V"]
            S_["v_ln2"] = C["V"]
            state["s_xhat"] = C["S"]
            S_["s_xhat"] = C["S"]
        th(ln2_apply)

        for sblk in range(3):
            l1, l2 = 2 * sblk, 2 * sblk + 1

            def lb_load(li, key):
                p.wait("sync", "wgp", WGP_ALL)
                p.wait("sync", "T", S_["t_lb"])
                p.emit("sync", lambda s_, l2_=li: s_.dma_start(out=lb_bf[0:R, 0, :], in_=lbT_sc[l2_]), [("wst", 16)])
                state[key] = C["wst"]
            th(lambda li=l1, sb=sblk: lb_load(li, ("lb", sb, 0)))

            def linear(lin, src_bf, lb_slot, out_kind, sb, upd_h=False):
                def loraA(lin=lin, src_bf=src_bf, sb=sb):
                    p.wait("tensor", "S", S_["bank_free_S"][2])
                    p.wait("tensor", "S", state["s_xhat"])
                    if src_bf is gbf:
                        p.wait("tensor", "S", state[("gelu", sb)])
                    p.emit("tensor", lambda t_: t_.nop())
                    for kt in range(DH):
                        off = (lin * 8 + kt) * R
                        p.emit("tensor", lambda t_, k=kt, o=off, sf=src_bf: t_.matmul(
                            out=pb[2][0:R, :], lhsT=la_bf[:, o:o + R], rhs=sf[:, k, :],
                            start=(k == 0), stop=(k == DH - 1)), [("T", 1)] if kt == DH - 1 else ())
                    state[("lorA", lin)] = C["T"]
                th(loraA)

                def xaev(lin=lin):
                    p.wait("scalar", "T", state[("lorA", lin)])
                    p.emit("scalar", lambda s_: s_.copy(out=xabf[0:R, :], in_=pb[2][0:R, :]), [("S", 1)])
                    state[("xa", lin)] = C["S"]
                    S_["bank_free_S"][2] = C["S"]
                th(xaev)

                for wv in range(2):
                    def waveT(lin=lin, wv=wv, src_bf=src_bf, lb_slot=lb_slot, out_kind=out_kind, sb=sb, upd_h=upd_h):
                        for sname, sval in dq_lin_done[lin]:
                            p.wait("sync", sname, sval)
                        p.wait("sync", "T", S_["t_wklast"])
                        wivals = []
                        for kt in range(DH):
                            wqs = "wsA" if kt % 2 == 0 else "wsB"
                            p.emit("sync", lambda s_, li=lin, k=kt, w=wv: s_.dma_start(
                                out=wk[:, k, 0:TC], in_=wffn_sc[li, w, k]), [(wqs, 16)])
                            wivals.append((wqs, C[wqs]))
                        p.wait("tensor", "S", state[("xa", lin)])
                        p.wait("tensor", "S", state["s_xhat"])
                        for kt in range(DH):
                            p.wait("tensor", wivals[kt][0], wivals[kt][1])
                            for j in range(4):
                                bank = j
                                if kt == 0:
                                    p.wait("tensor", "S", S_["bank_free_S"][bank])
                                    p.wait("tensor", "V", S_["bank_free_V"][bank])
                                p.emit("tensor", lambda t_, k=kt, jj=j, b=bank, sf=src_bf: t_.matmul(
                                    out=pb[b][:], lhsT=wk[:, k, jj * 128:(jj + 1) * 128], rhs=sf[:, k, :],
                                    start=(k == 0), stop=False), [("T", 1)] if (j == 3) else ())
                        p.wait("tensor", "wst", state[("lb", sb, lb_slot)])
                        for j in range(4):
                            m = wv * 4 + j
                            bank = j
                            p.emit("tensor", lambda t_, b=bank, mm=m: t_.matmul(
                                out=pb[b][:], lhsT=lb_bf[:, 0, mm * 128:(mm + 1) * 128], rhs=xabf[:],
                                start=False, stop=True), [("T", 1)])
                            state[("tl", lin, wv, j)] = C["T"]
                        if wv == 1:
                            S_["t_lb"] = C["T"]
                        state[("linT", lin)] = C["T"]
                        S_["t_wklast"] = C["T"]
                    th(waveT)

                    def waveS(lin=lin, wv=wv, out_kind=out_kind, sb=sb, upd_h=upd_h):
                        for j in range(4):
                            m = wv * 4 + j
                            bank = j
                            p.wait("scalar", "T", state[("tl", lin, wv, j)])
                            if out_kind == "y1":
                                p.emit("scalar", lambda s_, b=bank, mm=m, li=lin: s_.activation(
                                    out=y18[:, mm, :], in_=pb[b][:], func=ACT.Identity, bias=col(OF_BIAS + li * 8 + mm)), [("S", 1)])
                                state[("y1", lin, m)] = C["S"]
                            else:
                                p.wait("scalar", "G", state.get(("res", sb, m - 2), 0))
                                p.emit("scalar", lambda s_, b=bank, mm=m, li=lin: s_.activation(
                                    out=gvro[:, mm % 2, :], in_=pb[b][:], func=ACT.Identity, bias=col(OF_BIAS + li * 8 + mm)), [("S", 1)])
                                sev = C["S"]
                                p.wait("gpsimd", "S", sev)
                                p.wait("gpsimd", "V", state["v_ln2"])
                                if upd_h:
                                    p.emit("gpsimd", lambda g, mm=m: g.tensor_tensor(
                                        out=xs(mm), in0=xs(mm), in1=gvro[:, mm % 2, :], op=ALU.add))
                                    p.emit("gpsimd", lambda g, mm=m: g.tensor_tensor(
                                        out=xhat[:, mm, :], in0=xhat[:, mm, :], in1=gvro[:, mm % 2, :], op=ALU.add), [("G", 1)])
                                else:
                                    p.emit("gpsimd", lambda g, mm=m: g.tensor_tensor(
                                        out=xs(mm), in0=xs(mm), in1=gvro[:, mm % 2, :], op=ALU.add), [("G", 1)])
                                state[("res", sb, m)] = C["G"]
                            S_["bank_free_S"][bank] = C["S"]
                    th(waveS)

            linear(l1, xhat, 0, "y1", sblk)

            def ffn_lnA(sb=sblk, lin=l1):
                p.wait("scalar", "T", state[("linT", lin - 1)] if lin >= 2 else S_["t_blast"])
                for m in range(DH):
                    p.emit("scalar", lambda s_, mm=m: s_.activation(
                        out=gbf[:, mm, :], in_=y18[:, mm, :], func=ACT.Square),
                        [("S", 1)] if m == DH - 1 else ())
                gsq = C["S"]
                p.wait("tensor", "S", gsq)
                p.wait("tensor", "S", S_["bank_free_S"][0])
                p.wait("tensor", "S", S_["bank_free_S"][1])
                for kt in range(DH):
                    p.emit("tensor", lambda t_, k=kt: t_.matmul(out=pb[0][:], lhsT=onesb[:], rhs=y18[:, k, :], start=(k == 0), stop=(k == DH - 1)))
                    p.emit("tensor", lambda t_, k=kt: t_.matmul(out=pb[1][:], lhsT=onesb[:], rhs=gbf[:, k, :], start=(k == 0), stop=(k == DH - 1)), [("T", 1)] if kt == DH - 1 else ())
                state[("lnT", sb)] = C["T"]
            th(ffn_lnA)

            def ffn_lnB(sb=sblk):
                p.wait("scalar", "T", state[("lnT", sb)])
                p.wait("scalar", "V", state["v_ln2"])
                p.emit("scalar", lambda s_: s_.copy(out=mub[:], in_=pb[0][:]))
                p.emit("scalar", lambda s_: s_.copy(out=s2sb[:], in_=pb[1][:]))
                p.emit("scalar", lambda s_: s_.activation(out=rbB[:], in_=pb[0][:], func=ACT.Square), [("S", 1)])
                sev = C["S"]
                S_["bank_free_S"][0] = sev
                S_["bank_free_S"][1] = sev
                p.wait("gpsimd", "S", sev)
                p.emit("gpsimd", lambda g: g.tensor_tensor(out=gq[:], in0=s2sb[:], in1=rbB[:], op=ALU.subtract), [("G", 1)])
                gv = C["G"]
                p.wait("scalar", "G", gv)
                p.emit("scalar", lambda s_: s_.activation(out=rbB[:], in_=gq[:], func=ACT.Ln, bias=col(OF_EPS)))
                p.emit("scalar", lambda s_: s_.activation(out=rbB[:], in_=rbB[:], func=ACT.Exp, scale=-0.5), [("S", 1)])
                state[("rb", sb)] = C["S"]
            th(ffn_lnB)

            def ffn_apply_q(sb, q):
                p.wait("gpsimd", "S", state[("rb", sb)])
                for m in range(q * 2, q * 2 + 2):
                    p.emit("gpsimd", lambda g, mm=m: g.tensor_tensor(out=gvro[:, mm % 2, :], in0=y18[:, mm, :], in1=mub[:], op=ALU.subtract))
                    p.emit("gpsimd", lambda g, mm=m: g.tensor_tensor(out=gbf[:, mm, :], in0=gvro[:, mm % 2, :], in1=rbB[:], op=ALU.mult),
                           [("G", 1)] if m % 2 == 1 else ())
                state[("app", sb, q)] = C["G"]

            def ffn_gelu(sb):
                p.wait("scalar", "G", state[("app", sb, 3)])
                for m in range(DH):
                    p.emit("scalar", lambda s_, mm=m, sb_=sb: s_.activation(
                        out=gbf[:, mm, :], in_=gbf[:, mm, :], func=ACT.Gelu,
                        scale=col(OF_G + sb_ * 8 + mm), bias=col(OF_B + sb_ * 8 + mm)), [("S", 1)])
                state[("gelu", sb)] = C["S"]
            for q_ in range(4):
                th(lambda sb=sblk, qq=q_: ffn_apply_q(sb, qq))
            th(lambda sb=sblk: ffn_gelu(sb))
            th(lambda li=l2, sb=sblk: lb_load(li, ("lb", sb, 1)))

            linear(l2, gbf, 1, "resid", sblk, upd_h=(sblk < 2))

        def fin():
            g_last = C["G"]
            p.wait("sync", "G", g_last)
            p.wait("sync", "V", state["v_ln2"])
            p.emit("sync", lambda s_, cc=c: s_.dma_start(out=o_r[:, :, cc * TC:(cc + 1) * TC], in_=x8[:, par:par + DH, :]), [("sout", 16)])
            S_["t_blast"] = C["T"]
            S_["g_blast"] = C["G"]
            if c + 2 < nch:
                p.wait("sync", "sout", 16 * (c + 1))
                p.emit("sync", lambda s_, cc=c + 2: s_.dma_start(
                    out=x8[:, (cc % 2) * DH:(cc % 2) * DH + DH, :], in_=x_r[:, :, cc * TC:(cc + 1) * TC]), [("xin", 16)])
        th(fin)
        return thunks

    for t in make_A1(0):
        t()
    for t in _mk_z(0):
        t()
    p.emit("sync", lambda s_: s_.dma_start(out=x8[:, DH:2 * DH, :], in_=x_r[:, :, TC:2 * TC]), [("xin", 16)])

    for c in range(nch):
        if c >= 1:
            emit_wout(c - 1)
        emit_delta(c)
        if c + 1 < nch:
            pending.extend(make_A1(c + 1))
        if c >= 1:
            pending[0:0] = _mk_z(c)
        emit_scan(c)
        pending.extend(make_B(c))
    emit_wout(nch - 1)
    flush()
    while dq_pending:
        dq_pending.pop(0)()

    p.wait("sync", "sout", 16 * nch)
    for s in ("xin", "wcol", "wst", "wsA", "wsB", "wq0", "wq1", "dqA", "dqB", "wiA", "wiB"):
        p.wait("sync", s, C[s])
    p.wait("gpsimd", "wgp", WGP_ALL)

    with nc.Block() as block:
        def mk(engname):
            def run(eng):
                for item in p.q[engname]:
                    if item[0] == "w":
                        eng.wait_ge(sem[item[1]], item[2])
                        if engname == "tensor":
                            for _ in range(8):
                                eng.nop()
                    else:
                        ins = item[1](eng)
                        for s, a in item[2]:
                            ins.then_inc(sem[s], a)
            return run
        block.sync(mk("sync"))
        block.scalar(mk("scalar"))
        block.vector(mk("vector"))
        block.tensor(mk("tensor"))
        block.gpsimd(mk("gpsimd"))

    st.close()
    return nc


def _stage(inputs):
    import ml_dtypes
    x = np.asarray(inputs["L0_attn_x"], dtype=np.float32)
    W_in = np.ascontiguousarray(np.asarray(inputs["L0_attn_W_in"], dtype=np.float32))
    conv_w = np.asarray(inputs["L0_attn_conv_w"], dtype=np.float32).reshape(D, 4)
    conv_b = np.asarray(inputs["L0_attn_conv_b"], dtype=np.float32)
    W_xp = np.ascontiguousarray(np.asarray(inputs["L0_attn_W_xproj"], dtype=np.float32))
    log_A = np.asarray(inputs["L0_attn_log_A"], dtype=np.float32)
    D_param = np.asarray(inputs["L0_attn_D_param"], dtype=np.float32)
    W_out = np.ascontiguousarray(np.asarray(inputs["L0_attn_W_out"], dtype=np.float32))
    q = np.asarray(inputs["L0_ffn_qweights"], dtype=np.int32)
    scales = np.asarray(inputs["L0_ffn_scales"], dtype=np.float32)
    biases = np.asarray(inputs["L0_ffn_biases"], dtype=np.float32)
    la = np.asarray(inputs["L0_ffn_lora_a"], dtype=np.float32)
    lb = np.asarray(inputs["L0_ffn_lora_b"], dtype=np.float32)
    g = np.asarray(inputs["L0_ffn_ln_gamma"], dtype=np.float32)
    b = np.asarray(inputs["L0_ffn_ln_beta"], dtype=np.float32)

    cols = np.zeros((128, COLS_W), dtype=np.float32)
    cols[:, OF_CONV:OF_CONV + 32] = conv_w.reshape(DH, 128, 4).transpose(1, 0, 2).reshape(128, 32)
    cols[:, OF_CB:OF_CB + 8] = conv_b.reshape(DH, 128).T
    cols[:, OF_D:OF_D + 8] = D_param.reshape(DH, 128).T
    cols[:, OF_BIAS:OF_BIAS + 48] = biases.reshape(6, DH, 128).transpose(2, 0, 1).reshape(128, 48)
    cols[:, OF_G:OF_G + 24] = g.reshape(3, DH, 128).transpose(2, 0, 1).reshape(128, 24)
    cols[:, OF_B:OF_B + 24] = b.reshape(3, DH, 128).transpose(2, 0, 1).reshape(128, 24)
    cols[:, OF_ONE] = 1.0
    cols[:, OF_EPS] = 1e-5
    cols[:, OF_LOGA:OF_LOGA + 128] = log_A.reshape(DH, 128, NST).transpose(1, 0, 2).reshape(128, 128)

    sel = np.zeros((128, 16 * 128), dtype=np.float32)
    for n in range(NST):
        sel[n, n * 128:(n + 1) * 128] = 1.0
    sel = sel.astype(ml_dtypes.bfloat16)
    zf32 = np.zeros((128, 1024), dtype=np.float32)
    zbf = np.zeros((128, 1024), dtype=np.float32).astype(ml_dtypes.bfloat16)
    zone_np = np.zeros((128, 128), dtype=np.float32)
    zone_np[0, :] = 1.0
    onesb_np = np.full((128, 128), 1.0 / 1024.0, dtype=np.float32).astype(ml_dtypes.bfloat16)

    qT = np.ascontiguousarray(q.transpose(0, 2, 1))
    sT = np.ascontiguousarray(np.repeat(scales, 16, axis=2).transpose(0, 2, 1))
    laT = np.ascontiguousarray(la.transpose(0, 2, 1))
    lbT = np.ascontiguousarray(lb.transpose(0, 2, 1))

    shared = dict(w_in=W_in, w_xp=W_xp, w_out=W_out, cols=cols, selB=sel,
                  zf32=zf32, zbf=zbf, zone_in=zone_np, onesb_in=onesb_np,
                  qT=qT, sT=sT, laT=laT, lbT=lbT)
    in_maps = []
    for bidx in range(8):
        m = dict(shared)
        m["x_fm"] = np.ascontiguousarray(x[bidx].T)
        in_maps.append(m)
    return in_maps


def kernel(**inputs):
    in_maps = _stage(inputs)
    nc = build_program()
    res = run_bass_kernel_spmd(nc, in_maps, core_ids=list(range(8)))
    out = np.stack([np.asarray(r["out_fm"]).T for r in res.results], axis=0)
    return np.ascontiguousarray(out.astype(np.float32))


# version 23
# speedup vs baseline: 1.1026x; 1.1026x over previous
import numpy as np
from contextlib import ExitStack

import concourse.bass as bass
import concourse.mybir as mybir
from concourse.bass_utils import run_bass_kernel_spmd

F32 = mybir.dt.float32
BF16 = mybir.dt.bfloat16
I32 = mybir.dt.int32
ALU = mybir.AluOpType
ACT = mybir.ActivationFunctionType

T = 4096
D = 1024
DH = 8
NST = 16
TC = 512
NCH = T // TC
R = 32

ENGINES = ("sync", "scalar", "vector", "tensor", "gpsimd")
SEMS = ("xin", "wcol", "wst", "sout", "T", "S", "V", "G",
        "wsA", "wsB", "wq0", "wq1", "dqA", "dqB", "wgp", "wiA", "wiB")

OF_CONV = 0
OF_CB = 32
OF_D = 40
OF_BIAS = 48
OF_G = 96
OF_B = 120
OF_ONE = 144
OF_EPS = 145
OF_LOGA = 152
COLS_W = 280


class Plan:
    def __init__(self):
        self.q = {e: [] for e in ENGINES}
        self.cnt = {s: 0 for s in SEMS}
        self.maxw = {e: {s: 0 for s in SEMS} for e in ENGINES}

    def wait(self, e, sem, val):
        if val <= 0 or val <= self.maxw[e][sem]:
            return
        self.maxw[e][sem] = val
        self.q[e].append(("w", sem, val))

    def emit(self, e, fn, incs=()):
        self.q[e].append(("i", fn, incs))
        for s, a in incs:
            self.cnt[s] += a


def build_program(nch=NCH):
    nc = bass.Bass("TRN2", target_bir_lowering=False, detect_race_conditions=False)

    x_in = nc.dram_tensor("x_fm", [D, T], F32, kind="ExternalInput").ap()
    w_in = nc.dram_tensor("w_in", [D, 2 * D], F32, kind="ExternalInput").ap()
    w_xp = nc.dram_tensor("w_xp", [D, 33], F32, kind="ExternalInput").ap()
    w_out = nc.dram_tensor("w_out", [D, D], F32, kind="ExternalInput").ap()
    cols_in = nc.dram_tensor("cols", [128, COLS_W], F32, kind="ExternalInput").ap()
    sel_in = nc.dram_tensor("selB", [128, 16 * 128], BF16, kind="ExternalInput").ap()
    zf32_in = nc.dram_tensor("zf32", [128, 1024], F32, kind="ExternalInput").ap()
    zbf_in = nc.dram_tensor("zbf", [128, 1024], BF16, kind="ExternalInput").ap()
    zone_in = nc.dram_tensor("zone_in", [128, 128], F32, kind="ExternalInput").ap()
    onesb_in = nc.dram_tensor("onesb_in", [128, 128], BF16, kind="ExternalInput").ap()
    qT_in = nc.dram_tensor("qT", [6, D, D], I32, kind="ExternalInput").ap()
    sT_in = nc.dram_tensor("sT", [6, D, D], F32, kind="ExternalInput").ap()
    laT_in = nc.dram_tensor("laT", [6, D, R], F32, kind="ExternalInput").ap()
    lbT_in = nc.dram_tensor("lbT", [6, R, D], F32, kind="ExternalInput").ap()
    out_fm = nc.dram_tensor("out_fm", [D, T], F32, kind="ExternalOutput").ap()
    wffn_sc = nc.dram_tensor("wffn_sc", [6, 2, DH, 128, TC], BF16, kind="Internal").ap()
    wout_sc = nc.dram_tensor("wout_sc", [DH, 128, D], BF16, kind="Internal").ap()
    win_sc = nc.dram_tensor("win_sc", [4, DH, 128, TC], BF16, kind="Internal").ap()
    lbT_sc = nc.dram_tensor("lbT_sc", [6, R, D], BF16, kind="Internal").ap()

    st = ExitStack()
    en = st.enter_context

    def sbuf(name, shape, dt):
        return en(nc.sbuf_tensor(name, shape, dt))

    cols = sbuf("colsb", [128, COLS_W], F32)
    lam = sbuf("lam", [128, 128], F32)
    selB = sbuf("selBb", [128, 16 * 128], BF16)
    zone = sbuf("zone", [128, 128], F32)
    onesb = sbuf("onesb", [128, 128], BF16)
    wx_bf = sbuf("wx_bf", [128, DH, 33], BF16)
    la_bf = sbuf("la_bf", [128, 6 * DH * R], BF16)
    lb_bf = sbuf("lb_bf", [128, 2, D], BF16)
    winr = sbuf("winr", [128, 4, TC], BF16)
    wk = sbuf("wk", [128, 4, TC], BF16)
    qstg = sbuf("qstg", [128, 1, TC], I32)
    sstg = sbuf("sstg", [128, 1, TC], F32)
    gq = sbuf("gq", [128, TC], F32)
    dqo = sbuf("dqo", [128, 1, TC], BF16)

    x8 = sbuf("x8", [128, 2 * DH, TC], F32)
    xmix = sbuf("xmix", [128, DH, TC], BF16)
    xhat = sbuf("xhat", [128, DH, TC], BF16)
    cast2 = sbuf("cast2", [128, 4, TC], BF16)
    sq2 = sbuf("sq2", [128, 4, TC], BF16)
    trot = sbuf("trot", [128, 2, TC], F32)
    xm = sbuf("xm", [128, 2, 3 + TC], F32)
    hist = sbuf("hist", [128, 24], F32)
    vrot = sbuf("vrot", [128, 2, TC], F32)
    gz = sbuf("gz", [128, DH, TC], BF16)
    u8 = sbuf("u8", [128, DH, TC], F32)
    ubf = sbuf("ubf", [128, 2, TC], BF16)
    Xb = sbuf("Xb", [128, TC], BF16)
    Xc = sbuf("Xc", [128, TC], BF16)
    dsb = sbuf("dsb", [128, TC], F32)
    dbsb = sbuf("dbsb", [128, TC], F32)
    bbn = sbuf("bbn", [128, 2, TC], F32)
    arot = sbuf("arot", [128, 3, TC], F32)
    brot = sbuf("brot", [128, 3, TC], F32)
    hrot = sbuf("hrot", [128, 3, TC], BF16)
    cbn = sbuf("cbn", [128, 4, TC], BF16)
    carry = sbuf("carry", [128, 128], F32)
    yacc = sbuf("yacc", [128, DH, TC], F32)
    ygbf = sbuf("ygbf", [128, DH, TC], BF16)
    rbsb = sbuf("rbsb", [128, TC], F32)
    y18 = sbuf("y18", [128, DH, TC], BF16)
    gbf = sbuf("gbf", [128, DH, TC], BF16)
    xabf = sbuf("xabf", [128, TC], BF16)
    sq2B = sbuf("sq2B", [128, 2, TC], BF16)
    mub = sbuf("mub", [128, TC], F32)
    s2sb = sbuf("s2sb", [128, TC], F32)
    rbB = sbuf("rbB", [128, TC], F32)
    gvro = sbuf("gvro", [128, 2, TC], F32)

    pb = [en(nc.psum_tensor(f"pb{i}", [128, TC], F32)) for i in range(8)]

    sem = {s: en(nc.semaphore(s)) for s in SEMS}

    p = Plan()
    C = p.cnt

    def col(off):
        return cols[:, off:off + 1]

    x_r = x_in.rearrange("(dh p) t -> p dh t", p=128)
    o_r = out_fm.rearrange("(dh p) t -> p dh t", p=128)

    for dst, srcap in [
        (lambda: cols[:], cols_in), (lambda: selB[:], sel_in),
        (lambda: zone[:], zone_in), (lambda: onesb[:], onesb_in),
        (lambda: carry[:], zf32_in[:, 0:128]), (lambda: hist[:], zf32_in[:, 0:24]),
        (lambda: dsb[:], zf32_in[:, 0:TC]), (lambda: Xb[:], zbf_in[:, 0:TC]),
        (lambda: Xc[:], zbf_in[:, 0:TC]), (lambda: xabf[:], zbf_in[:, 0:TC]),
        (lambda: lb_bf[:, 0, :], zbf_in), (lambda: lb_bf[:, 1, :], zbf_in),
    ]:
        p.emit("sync", lambda s_, d=dst, sa=srcap: s_.dma_start(out=d(), in_=sa), [("wcol", 16)])
    p.emit("sync", lambda s_: s_.dma_start(out=x8[:, 0:DH, :], in_=x_r[:, :, 0:TC]), [("xin", 16)])
    WCOL = C["wcol"]
    p.wait("scalar", "wcol", WCOL)
    p.emit("scalar", lambda s_: s_.activation(out=lam[:], in_=cols[:, OF_LOGA:OF_LOGA + 128], func=ACT.Exp), [("S", 1)])
    p.wait("vector", "S", C["S"])
    p.emit("vector", lambda v: v.tensor_scalar(out=lam[:], in0=lam[:], scalar1=-1.0, scalar2=None, op0=ALU.mult), [("V", 1)])

    w_in_r = w_in.rearrange("(dh p) m -> dh p m", p=128)
    w_xp_r = w_xp.rearrange("(dh p) m -> p dh m", p=128)
    la_r = laT_in.rearrange("l (kt p) r -> p l kt r", p=128)
    la_bf_v = la_bf[:].rearrange("p (l kt r) -> p l kt r", l=6, kt=DH)
    w_out_r = w_out.rearrange("(dh p) m -> dh p m", p=128)
    for q in range(4):
        p.emit("gpsimd", lambda g, qq=q: g.dma_start(
            out=win_sc[qq], in_=w_in_r[:, :, qq * TC:(qq + 1) * TC]), [("wgp", 16)])
    p.emit("gpsimd", lambda g: g.dma_start(out=wx_bf[:], in_=w_xp_r), [("wgp", 16)])
    p.emit("gpsimd", lambda g: g.dma_start(out=la_bf_v, in_=la_r), [("wgp", 16)])
    p.emit("gpsimd", lambda g: g.dma_start(out=wout_sc[:], in_=w_out_r), [("wgp", 16)])
    p.emit("gpsimd", lambda g: g.dma_start(out=lbT_sc[:], in_=lbT_in), [("wgp", 16)])
    WGP_ALL = C["wgp"]

    dq_pending = []
    dq_lin_done = []
    dq_state = {"npc": 0, "g": {}, "st": []}

    def mk_dq(l, kt, hf):
        def run():
            npc = dq_state["npc"]
            dqs = "dqA" if npc % 2 == 0 else "dqB"
            if npc >= 1:
                p.wait("sync", "G", dq_state["g"][npc - 1])
            p.emit("sync", lambda s_: s_.dma_start(
                out=qstg[:, 0, :], in_=qT_in[l, kt * 128:(kt + 1) * 128, hf * TC:(hf + 1) * TC]), [(dqs, 16)])
            p.emit("sync", lambda s_: s_.dma_start(
                out=sstg[:, 0, :], in_=sT_in[l, kt * 128:(kt + 1) * 128, hf * TC:(hf + 1) * TC]), [(dqs, 16)])
            p.wait("gpsimd", dqs, C[dqs])
            if npc >= 1:
                p.wait("gpsimd", dq_state["st"][npc - 1][0], dq_state["st"][npc - 1][1])
            p.emit("gpsimd", lambda g: g.tensor_scalar(
                out=gq[:], in0=qstg[:, 0, :], scalar1=1.0 / 7.5, scalar2=-1.0, op0=ALU.mult, op1=ALU.add))
            p.emit("gpsimd", lambda g: g.tensor_tensor(
                out=dqo[:, 0, :], in0=gq[:], in1=sstg[:, 0, :], op=ALU.mult), [("G", 1)])
            dq_state["g"][npc] = C["G"]
            p.wait("sync", "G", C["G"])
            wqs = "wq0" if npc % 2 == 0 else "wq1"
            p.emit("sync", lambda s_: s_.dma_start(out=wffn_sc[l, hf, kt], in_=dqo[:, 0, :]), [(wqs, 16)])
            dq_state["st"].append((wqs, C[wqs]))
            dq_state["npc"] = npc + 1
        return run

    nq0 = nq1 = 0
    for l in range(6):
        for kt in range(DH):
            for hf in range(2):
                dq_pending.append(mk_dq(l, kt, hf))
                if (l * 16 + kt * 2 + hf) % 2 == 0:
                    nq0 += 1
                else:
                    nq1 += 1
        dq_lin_done.append([("wq0", 16 * nq0), ("wq1", 16 * nq1)])
    G_DQ_END = 0

    S_ = {
        "v_done127": 0, "g_done127": 0,
        "tbb15": 0, "tcb15": 0, "sbb15": 0, "scb15": 0,
        "t_dltab": 0, "vgate": 0, "vconv7": 0, "v_ln2": 0,
        "s_xcevac": 0, "t_wout": 0, "t_blast": 0, "g_blast": 0,
        "bank_free_S": [0] * 8, "bank_free_V": [0] * 8,
        "s_carry_all": 0, "g_b_last": G_DQ_END, "v_yinit": 0,
        "t_lb": 0, "s_xhat": 0,
    }

    def stats_pair(src_getter, bank_mu, bank_s2, bf_direct=False):
        s0 = C["S"]
        tb0 = C["T"]
        for kt in range(DH):
            if bf_direct:
                if kt >= 2:
                    p.wait("scalar", "T", tb0 + kt - 1)
            elif kt >= 4:
                p.wait("scalar", "T", tb0 + kt - 3)
            if not bf_direct:
                p.emit("scalar", lambda s_, k=kt: s_.copy(out=cast2[:, k % 4, :], in_=src_getter(k)))
                p.emit("scalar", lambda s_, k=kt: s_.activation(out=sq2[:, k % 4, :], in_=src_getter(k), func=ACT.Square), [("S", 1)])
                p.wait("tensor", "S", s0 + kt + 1)
                p.emit("tensor", lambda t_, k=kt, b=bank_mu: t_.matmul(out=pb[b][:], lhsT=onesb[:], rhs=cast2[:, k % 4, :], start=(k == 0), stop=(k == DH - 1)))
                p.emit("tensor", lambda t_, k=kt, b=bank_s2: t_.matmul(out=pb[b][:], lhsT=onesb[:], rhs=sq2[:, k % 4, :], start=(k == 0), stop=(k == DH - 1)), [("T", 1)])
            else:
                p.emit("scalar", lambda s_, k=kt: s_.activation(out=sq2B[:, k % 2, :], in_=src_getter(k), func=ACT.Square), [("S", 1)])
                p.wait("tensor", "S", s0 + kt + 1)
                p.emit("tensor", lambda t_, k=kt, b=bank_mu: t_.matmul(out=pb[b][:], lhsT=onesb[:], rhs=src_getter(k), start=(k == 0), stop=(k == DH - 1)))
                p.emit("tensor", lambda t_, k=kt, b=bank_s2: t_.matmul(out=pb[b][:], lhsT=onesb[:], rhs=sq2B[:, k % 2, :], start=(k == 0), stop=(k == DH - 1)), [("T", 1)])
        return C["T"]

    def rb_from_stats(bank_mu, bank_s2, rb_dst):
        tstats = C["T"]
        p.wait("scalar", "T", tstats)
        p.emit("scalar", lambda s_, b=bank_mu: s_.activation(out=trot[:, 0, :], in_=pb[b][:], func=ACT.Square), [("S", 1)])
        p.wait("vector", "S", C["S"])
        p.wait("vector", "T", tstats)
        p.emit("vector", lambda v, b=bank_s2: v.tensor_tensor(out=vrot[:, 0, :], in0=pb[b][:], in1=trot[:, 0, :], op=ALU.subtract), [("V", 1)])
        p.wait("scalar", "V", C["V"])
        p.emit("scalar", lambda s_, r=rb_dst: s_.activation(out=r[:], in_=vrot[:, 0, :], func=ACT.Ln, bias=col(OF_EPS)))
        p.emit("scalar", lambda s_, r=rb_dst: s_.activation(out=r[:], in_=r[:], func=ACT.Exp, scale=-0.5), [("S", 1)])
        return C["S"]

    def emit_A(c, pump, flush):
        par = (c % 2) * DH

        def xs(dh):
            return x8[:, par + dh, :]

        for e in ("vector", "scalar", "tensor"):
            p.wait(e, "xin", 16 * (c + 1))
            p.wait(e, "wcol", WCOL)
        p.wait("tensor", "wgp", WGP_ALL)
        p.wait("tensor", "S", S_["s_xcevac"])
        p.wait("tensor", "S", S_["sbb15"])
        p.wait("tensor", "V", S_["v_ln2"])
        stats_pair(xs, 4, 5)
        srb = rb_from_stats(4, 5, rbsb)
        for kt in range(DH):
            p.wait("vector", "S", srb)
            p.emit("vector", lambda v, k=kt: v.tensor_tensor(out=trot[:, k % 2, :], in0=xs(k), in1=pb[4][:], op=ALU.subtract))
            p.emit("vector", lambda v, k=kt: v.tensor_tensor(out=xmix[:, k, :], in0=trot[:, k % 2, :], in1=rbsb[:], op=ALU.mult), [("V", 1)])
        v_ln1 = C["V"]
        pump()

        sevac = {}
        vconv = {}
        ssilu = {}
        for q in range(4):
            wis = "wiA" if q % 2 == 0 else "wiB"
            tq0 = C["T"]
            wivals = []
            for kt in range(DH):
                if q == 0 and kt < 4:
                    p.wait("sync", "T", S_["t_wout"])
                p.wait("sync", "T", tq0 + kt - 3)
                p.emit("sync", lambda s_, qq=q, k=kt: s_.dma_start(out=winr[:, k % 4, :], in_=win_sc[qq, k]), [(wis, 16)])
                wivals.append((wis, C[wis]))
            p.wait("tensor", "V", v_ln1)
            for kt in range(DH):
                p.wait("tensor", wivals[kt][0], wivals[kt][1])
                for j in range(4):
                    bank = 4 + j
                    if kt == 0:
                        p.wait("tensor", "S", S_["bank_free_S"][bank])
                    p.emit("tensor", lambda t_, k=kt, jj=j, b=bank: t_.matmul(
                        out=pb[b][:], lhsT=winr[:, k % 4, jj * 128:(jj + 1) * 128], rhs=xmix[:, k, :],
                        start=(k == 0), stop=(k == DH - 1)),
                        [("T", 1)] if j == 3 else ())
            tq_done = C["T"]
            for j in range(4):
                m = q * 4 + j
                bank = 4 + j
                p.wait("scalar", "T", tq_done)
                if m < 8:
                    dh = m
                    sl = m % 2
                    if m >= 2:
                        p.wait("scalar", "V", vconv[m - 2])
                    else:
                        p.wait("scalar", "V", S_["vconv7"])
                    p.emit("scalar", lambda s_, d=dh, s3=sl: s_.copy(out=xm[:, s3, 0:3], in_=hist[:, d * 3:(d + 1) * 3]))
                    p.emit("scalar", lambda s_, d=dh, s3=sl, b=bank: s_.copy(out=xm[:, s3, 3:3 + TC], in_=pb[b][:]), [("S", 1)])
                    sevac[m] = C["S"]
                    S_["bank_free_S"][bank] = C["S"]
                    p.wait("vector", "S", sevac[m])
                    if m >= 2:
                        p.wait("vector", "S", ssilu.get(m - 2, 0))
                    p.emit("vector", lambda v, d=dh, s3=sl: v.tensor_scalar(
                        out=vrot[:, d % 2, :], in0=xm[:, s3, 0:TC], scalar1=col(OF_CONV + d * 4 + 0),
                        scalar2=col(OF_CB + d), op0=ALU.mult, op1=ALU.add))
                    for k in range(1, 4):
                        p.emit("vector", lambda v, d=dh, s3=sl, kk=k: v.scalar_tensor_tensor(
                            out=vrot[:, d % 2, :], in0=xm[:, s3, kk:kk + TC], scalar=col(OF_CONV + d * 4 + kk),
                            in1=vrot[:, d % 2, :], op0=ALU.mult, op1=ALU.add), [("V", 1)] if k == 3 else ())
                    vconv[m] = C["V"]
                    p.wait("scalar", "V", vconv[m])
                    p.emit("scalar", lambda s_, d=dh, s3=sl: s_.copy(out=hist[:, d * 3:(d + 1) * 3], in_=xm[:, s3, TC:TC + 3]))
                    if m == 0:
                        p.wait("scalar", "G", S_["g_done127"])
                        p.wait("scalar", "V", S_["v_yinit"])
                    p.emit("scalar", lambda s_, d=dh: s_.activation(out=u8[:, d, :], in_=vrot[:, d % 2, :], func=ACT.Silu), [("S", 1)])
                    ssilu[m] = C["S"]
                else:
                    dh = m - 8
                    if m == 8:
                        p.wait("scalar", "V", S_["vgate"])
                    p.emit("scalar", lambda s_, d=dh, b=bank: s_.activation(out=gz[:, d, :], in_=pb[b][:], func=ACT.Silu), [("S", 1)])
                    sevac[m] = C["S"]
                    S_["bank_free_S"][bank] = C["S"]
            pump()
        S_["vconv7"] = vconv[7]

        txp0 = C["T"]
        for kt in range(DH):
            if kt >= 2:
                p.wait("scalar", "T", txp0 + kt - 1)
            p.emit("scalar", lambda s_, k=kt: s_.copy(out=ubf[:, k % 2, :], in_=u8[:, k, :]), [("S", 1)])
            p.wait("tensor", "S", C["S"])
            if kt == 0:
                p.wait("tensor", "S", S_["s_xcevac"])
                p.wait("tensor", "S", S_["scb15"])
            p.emit("tensor", lambda t_, k=kt: t_.matmul(out=pb[4][0:16, :], lhsT=wx_bf[:, k, 0:16], rhs=ubf[:, k % 2, :],
                                                        start=(k == 0), stop=(k == DH - 1)))
            p.emit("tensor", lambda t_, k=kt: t_.matmul(out=pb[5][0:16, :], lhsT=wx_bf[:, k, 16:32], rhs=ubf[:, k % 2, :],
                                                        start=(k == 0), stop=(k == DH - 1)))
            p.emit("tensor", lambda t_, k=kt: t_.matmul(out=pb[6][0:1, :], lhsT=wx_bf[:, k, 32:33], rhs=ubf[:, k % 2, :],
                                                        start=(k == 0), stop=(k == DH - 1)), [("T", 1)])
        p.wait("scalar", "T", C["T"])
        p.wait("scalar", "T", S_["tcb15"])
        p.emit("scalar", lambda s_: s_.copy(out=Xc[0:16, :], in_=pb[5][0:16, :]))
        p.emit("scalar", lambda s_: s_.activation(out=rbsb[0:1, 0:TC], in_=pb[6][0:1, :], func=ACT.Abs))
        p.emit("scalar", lambda s_: s_.activation(out=rbsb[0:1, 0:TC], in_=rbsb[0:1, 0:TC], func=ACT.Exp, scale=-1.0))
        p.emit("scalar", lambda s_: s_.activation(out=rbsb[0:1, 0:TC], in_=rbsb[0:1, 0:TC], func=ACT.Ln, bias=cols[0:1, OF_ONE:OF_ONE + 1]))
        p.emit("scalar", lambda s_: s_.activation(out=dsb[0:1, 0:TC], in_=pb[6][0:1, :], func=ACT.Relu), [("S", 1)])
        S_["s_xcevac"] = C["S"]
        sxpe = C["S"]
        p.wait("vector", "S", sxpe)
        p.emit("vector", lambda v: v.tensor_tensor(out=dsb[0:1, 0:TC], in0=rbsb[0:1, 0:TC], in1=dsb[0:1, 0:TC], op=ALU.add), [("V", 1)])
        vdel = C["V"]
        p.wait("tensor", "V", vdel)
        p.wait("tensor", "S", S_["scb15"])
        p.emit("tensor", lambda t_: t_.matmul(out=pb[6][:], lhsT=zone[:], rhs=dsb[:, 0:TC], start=True, stop=True), [("T", 1)])
        S_["t_dltab"] = C["T"]
        p.wait("scalar", "T", C["T"])
        p.emit("scalar", lambda s_: s_.copy(out=dbsb[:], in_=pb[6][:]), [("S", 1)])
        sdb = C["S"]
        p.wait("vector", "S", sdb)
        p.wait("vector", "T", S_["tbb15"])
        p.emit("vector", lambda v: v.tensor_tensor(out=Xb[0:16, :], in0=pb[4][0:16, :], in1=dbsb[0:16, :], op=ALU.mult), [("V", 1)])
        vbp = C["V"]
        pump()

        for dh in range(DH):
            if dh == 0:
                p.wait("vector", "G", S_["g_done127"])
            p.emit("vector", lambda v, d=dh: v.tensor_scalar(
                out=yacc[:, d, :], in0=u8[:, d, :], scalar1=col(OF_D + d), scalar2=None, op0=ALU.mult),
                [("V", 1)] if dh == DH - 1 else ())
        S_["v_yinit"] = C["V"]
        v_done = {}
        g_done = {}
        sa_done = {}
        scarry = {}
        sbb = {}
        scb = {}
        tbb = {}
        tcb = {}
        prev_vd127 = S_["v_done127"]
        prev_sbb15 = S_["sbb15"]
        prev_scb15 = S_["scb15"]
        prev_gb = S_["g_b_last"]
        for n in range(NST):
            p.wait("tensor", "S", sbb[n - 1] if n >= 1 else prev_sbb15)
            p.wait("tensor", "V", vbp)
            p.emit("tensor", lambda t_, nn=n: t_.matmul(out=pb[5][:], lhsT=selB[:, nn * 128:(nn + 1) * 128], rhs=Xb[:], start=True, stop=True), [("T", 1)])
            tbb[n] = C["T"]
            p.wait("tensor", "S", scb[n - 2] if n >= 2 else prev_scb15)
            p.emit("tensor", lambda t_, nn=n: t_.matmul(out=pb[6 + nn % 2][:], lhsT=selB[:, nn * 128:(nn + 1) * 128], rhs=Xc[:], start=True, stop=True), [("T", 1)])
            tcb[n] = C["T"]
            p.wait("scalar", "T", tbb[n])
            p.wait("scalar", "G", g_done[(n - 1) * 8 - 1] if n >= 2 else prev_gb)
            p.emit("scalar", lambda s_, nn=n: s_.copy(out=bbn[:, nn % 2, :], in_=pb[5][:]), [("S", 1)])
            sbb[n] = C["S"]
            p.wait("scalar", "T", tcb[n])
            p.wait("scalar", "V", v_done[(n - 3) * 8 - 1] if n >= 4 else prev_vd127)
            p.emit("scalar", lambda s_, nn=n: s_.copy(out=cbn[:, nn % 4, :], in_=pb[6 + nn % 2][:]), [("S", 1)])
            scb[n] = C["S"]
            for dh in range(DH):
                i = n * 8 + dh
                if i >= 3:
                    p.wait("scalar", "V", v_done[i - 3])
                    p.emit("scalar", lambda s_, ii=i - 3: s_.copy(out=carry[:, ii:ii + 1], in_=hrot[:, ii % 3, TC - 1:TC]), [("S", 1)])
                    scarry[i - 3] = C["S"]
                p.emit("scalar", lambda s_, nn=n, d=dh, sl=i % 3: s_.activation(
                    out=arot[:, sl, :], in_=dbsb[:], func=ACT.Exp, scale=lam[:, d * 16 + nn:d * 16 + nn + 1]), [("S", 1)])
                sa_done[i] = C["S"]
                p.wait("gpsimd", "S", sbb[n])
                p.wait("gpsimd", "V", v_done[i - 3] if i >= 3 else prev_vd127)
                p.emit("gpsimd", lambda g, nn=n, d=dh, sl=i % 3: g.tensor_tensor(
                    out=brot[:, sl, :], in0=u8[:, d, :], in1=bbn[:, nn % 2, :], op=ALU.mult), [("G", 1)])
                g_done[i] = C["G"]
                p.wait("vector", "S", sa_done[i])
                p.wait("vector", "G", g_done[i])
                if i >= 3:
                    p.wait("vector", "S", scarry[i - 3])
                else:
                    p.wait("vector", "S", S_["s_carry_all"])
                p.emit("vector", lambda v, sl=i % 3, ii=i: v.tensor_tensor_scan(
                    out=hrot[:, sl, :], data0=arot[:, sl, :], data1=brot[:, sl, :],
                    initial=carry[:, ii:ii + 1], op0=ALU.mult, op1=ALU.add))
                p.wait("vector", "S", scb[n])
                p.emit("vector", lambda v, sl=i % 3, nn=n: v.tensor_tensor(
                    out=cast2[:, 0, :], in0=hrot[:, sl, :], in1=cbn[:, nn % 4, :], op=ALU.mult))
                p.emit("vector", lambda v, d=dh: v.tensor_tensor(
                    out=yacc[:, d, :], in0=yacc[:, d, :], in1=cast2[:, 0, :], op=ALU.add), [("V", 1)])
                v_done[i] = C["V"]
            pump()
        for ii in range(NST * 8 - 3, NST * 8):
            p.wait("scalar", "V", v_done[ii])
            p.emit("scalar", lambda s_, i2=ii: s_.copy(out=carry[:, i2:i2 + 1], in_=hrot[:, i2 % 3, TC - 1:TC]), [("S", 1)])
        S_["s_carry_all"] = C["S"]
        S_["v_done127"] = v_done[NST * 8 - 1]
        S_["g_done127"] = g_done[NST * 8 - 1]
        S_["g_b_last"] = g_done[NST * 8 - 1]
        S_["tbb15"] = tbb[NST - 1]
        S_["tcb15"] = tcb[NST - 1]
        S_["sbb15"] = sbb[NST - 1]
        S_["scb15"] = scb[NST - 1]
        S_["bank_free_S"][5] = sbb[NST - 1]
        S_["bank_free_S"][6] = max(S_["bank_free_S"][6], scb[NST - 2])
        S_["bank_free_S"][7] = max(S_["bank_free_S"][7], scb[NST - 1])

        flush()

        for dh in range(DH):
            p.emit("vector", lambda v, d=dh: v.tensor_tensor(out=ygbf[:, d, :], in0=yacc[:, d, :], in1=gz[:, d, :], op=ALU.mult),
                   [("V", 1)] if dh == DH - 1 else ())
        S_["vgate"] = C["V"]

        tH0 = C["T"]
        p.wait("sync", "wgp", WGP_ALL)
        p.wait("sync", "T", S_["t_blast"])
        wkt = []
        for kt in range(DH):
            p.wait("sync", "T", tH0 + kt - 3)
            wqs = "wsA" if kt % 2 == 0 else "wsB"
            p.emit("sync", lambda s_, k=kt: s_.dma_start(out=wk[:, k % 4, 0:TC], in_=wout_sc[k][:, 0:TC]), [(wqs, 16)])
            p.emit("sync", lambda s_, k=kt: s_.dma_start(out=winr[:, k % 4, 0:TC], in_=wout_sc[k][:, TC:2 * TC]), [(wqs, 16)])
            wkt.append((wqs, C[wqs]))
        p.wait("tensor", "V", S_["vgate"])
        for b in range(8):
            p.wait("tensor", "S", S_["bank_free_S"][b])
            p.wait("tensor", "V", S_["bank_free_V"][b])
        for kt in range(DH):
            p.wait("tensor", wkt[kt][0], wkt[kt][1])
            for m in range(DH):
                src = wk if m < 4 else winr
                mm_l = m if m < 4 else m - 4
                p.emit("tensor", lambda t_, k=kt, mm=m, s2=src, ml=mm_l: t_.matmul(
                    out=pb[mm][:], lhsT=s2[:, k % 4, ml * 128:(ml + 1) * 128], rhs=ygbf[:, k, :],
                    start=(k == 0), stop=(k == DH - 1)),
                    [("T", 1)] if m == DH - 1 else ())
        tHdone = C["T"]
        S_["t_wout"] = tHdone
        p.wait("vector", "T", tHdone)
        for m in range(DH):
            p.emit("vector", lambda v, mm=m: v.tensor_tensor(out=xs(mm), in0=xs(mm), in1=pb[mm][:], op=ALU.add),
                   [("V", 1)] if m == DH - 1 else ())
        vresid = C["V"]
        for m in range(DH):
            S_["bank_free_V"][m] = vresid

        p.wait("tensor", "G", S_["g_blast"])
        p.wait("tensor", "T", S_["t_blast"])
        p.wait("tensor", "V", vresid)
        p.wait("scalar", "V", vresid)
        stats_pair(xs, 4, 5)
        srb2 = rb_from_stats(4, 5, rbsb)
        sxh0 = C["S"]
        for kt in range(DH):
            p.wait("vector", "S", srb2 + max(0, kt - 1))
            p.emit("vector", lambda v, k=kt: v.tensor_tensor(out=trot[:, k % 2, :], in0=xs(k), in1=pb[4][:], op=ALU.subtract))
            p.emit("vector", lambda v, k=kt: v.tensor_tensor(out=vrot[:, k % 2, :], in0=trot[:, k % 2, :], in1=rbsb[:], op=ALU.mult))
            p.emit("vector", lambda v, k=kt: v.tensor_tensor(out=xs(k), in0=xs(k), in1=vrot[:, k % 2, :], op=ALU.add), [("V", 1)])
            p.wait("scalar", "V", C["V"])
            if kt == 0:
                p.wait("scalar", "G", S_["g_blast"])
            p.emit("scalar", lambda s_, k=kt: s_.copy(out=xhat[:, k, :], in_=vrot[:, k % 2, :]), [("S", 1)])
        bfree = C["S"]
        for b in (4, 5):
            S_["bank_free_S"][b] = bfree
        S_["v_ln2"] = C["V"]
        S_["s_xhat"] = bfree

    def make_B(c):
        par = (c % 2) * DH

        def xs(dh):
            return x8[:, par + dh, :]

        thunks = []

        def th(fn):
            thunks.append(fn)

        state = {}
        v_ln2_c = S_["v_ln2"]
        s_xhat_c = S_["s_xhat"]

        for sblk in range(3):
            l1, l2 = 2 * sblk, 2 * sblk + 1

            def lb_load(a=l1, b2=l2, sb=sblk):
                p.wait("sync", "wgp", WGP_ALL)
                p.wait("sync", "T", S_["t_lb"])
                p.emit("sync", lambda s_, li=a: s_.dma_start(out=lb_bf[0:R, 0, :], in_=lbT_sc[li]), [("wst", 16)])
                p.emit("sync", lambda s_, li=b2: s_.dma_start(out=lb_bf[0:R, 1, :], in_=lbT_sc[li]), [("wst", 16)])
                state[("lb", sb)] = C["wst"]
            th(lb_load)

            def linear(lin, src_bf, lb_slot, out_kind, sb, upd_h=False):
                def loraA(lin=lin, src_bf=src_bf, sb=sb):
                    p.wait("tensor", "S", S_["bank_free_S"][2])
                    p.wait("tensor", "S", s_xhat_c)
                    if src_bf is gbf:
                        p.wait("tensor", "S", state[("gelu", sb)])
                    p.emit("tensor", lambda t_: t_.nop())
                    for kt in range(DH):
                        off = (lin * 8 + kt) * R
                        p.emit("tensor", lambda t_, k=kt, o=off, sf=src_bf: t_.matmul(
                            out=pb[2][0:R, :], lhsT=la_bf[:, o:o + R], rhs=sf[:, k, :],
                            start=(k == 0), stop=(k == DH - 1)), [("T", 1)] if kt == DH - 1 else ())
                    state[("lorA", lin)] = C["T"]
                th(loraA)

                def xaev(lin=lin):
                    p.wait("scalar", "T", state[("lorA", lin)])
                    p.emit("scalar", lambda s_: s_.copy(out=xabf[0:R, :], in_=pb[2][0:R, :]), [("S", 1)])
                    state[("xa", lin)] = C["S"]
                    S_["bank_free_S"][2] = C["S"]
                th(xaev)

                for wv in range(2):
                    def wave(lin=lin, wv=wv, src_bf=src_bf, lb_slot=lb_slot, out_kind=out_kind, sb=sb, upd_h=upd_h):
                        t0w = C["T"]
                        for sname, sval in dq_lin_done[lin]:
                            p.wait("sync", sname, sval)
                        p.wait("sync", "T", S_["t_wout"])
                        wivals = []
                        for kt in range(DH):
                            p.wait("sync", "T", t0w + kt - 3)
                            wqs = "wsA" if kt % 2 == 0 else "wsB"
                            p.emit("sync", lambda s_, li=lin, k=kt, w=wv: s_.dma_start(
                                out=wk[:, k % 4, 0:TC], in_=wffn_sc[li, w, k]), [(wqs, 16)])
                            wivals.append((wqs, C[wqs]))
                        p.wait("tensor", "S", state[("xa", lin)])
                        p.wait("tensor", "S", s_xhat_c)
                        for kt in range(DH):
                            p.wait("tensor", wivals[kt][0], wivals[kt][1])
                            for j in range(4):
                                bank = j
                                if kt == 0:
                                    p.wait("tensor", "S", S_["bank_free_S"][bank])
                                p.emit("tensor", lambda t_, k=kt, jj=j, b=bank, sf=src_bf: t_.matmul(
                                    out=pb[b][:], lhsT=wk[:, k % 4, jj * 128:(jj + 1) * 128], rhs=sf[:, k, :],
                                    start=(k == 0), stop=False), [("T", 1)] if (j == 3) else ())
                        p.wait("tensor", "wst", state[("lb", sb)])
                        for j in range(4):
                            m = wv * 4 + j
                            bank = j
                            p.emit("tensor", lambda t_, b=bank, mm=m, ls=lb_slot: t_.matmul(
                                out=pb[b][:], lhsT=lb_bf[:, ls, mm * 128:(mm + 1) * 128], rhs=xabf[:],
                                start=False, stop=True), [("T", 1)])
                            tl = C["T"]
                            if out_kind == "resid" and wv == 1 and j == 3:
                                S_["t_lb"] = tl
                            p.wait("scalar", "T", tl)
                            if out_kind == "y1":
                                p.emit("scalar", lambda s_, b=bank, mm=m, li=lin: s_.activation(
                                    out=y18[:, mm, :], in_=pb[b][:], func=ACT.Identity, bias=col(OF_BIAS + li * 8 + mm)), [("S", 1)])
                                state[("y1", lin, m)] = C["S"]
                            else:
                                p.wait("scalar", "G", state.get(("res", sb, m - 2), 0))
                                p.emit("scalar", lambda s_, b=bank, mm=m, li=lin: s_.activation(
                                    out=gvro[:, mm % 2, :], in_=pb[b][:], func=ACT.Identity, bias=col(OF_BIAS + li * 8 + mm)), [("S", 1)])
                                sev = C["S"]
                                p.wait("gpsimd", "S", sev)
                                p.wait("gpsimd", "V", v_ln2_c)
                                if upd_h:
                                    p.emit("gpsimd", lambda g, mm=m: g.tensor_tensor(
                                        out=xs(mm), in0=xs(mm), in1=gvro[:, mm % 2, :], op=ALU.add))
                                    p.emit("gpsimd", lambda g, mm=m: g.tensor_tensor(
                                        out=xhat[:, mm, :], in0=xhat[:, mm, :], in1=gvro[:, mm % 2, :], op=ALU.add), [("G", 1)])
                                else:
                                    p.emit("gpsimd", lambda g, mm=m: g.tensor_tensor(
                                        out=xs(mm), in0=xs(mm), in1=gvro[:, mm % 2, :], op=ALU.add), [("G", 1)])
                                state[("res", sb, m)] = C["G"]
                            S_["bank_free_S"][bank] = C["S"]
                    th(wave)

            linear(l1, xhat, 0, "y1", sblk)

            def ffn_ln(sb=sblk, lin=l1):
                p.wait("tensor", "S", S_["bank_free_S"][0])
                p.wait("tensor", "S", S_["bank_free_S"][1])
                p.wait("tensor", "S", state[("y1", lin, 7)])
                tst = stats_pair(lambda k: y18[:, k, :], 0, 1, bf_direct=True)
                p.wait("scalar", "T", tst)
                p.emit("scalar", lambda s_: s_.copy(out=mub[:], in_=pb[0][:]))
                p.emit("scalar", lambda s_: s_.copy(out=s2sb[:], in_=pb[1][:]))
                p.emit("scalar", lambda s_: s_.activation(out=rbB[:], in_=pb[0][:], func=ACT.Square), [("S", 1)])
                sev = C["S"]
                S_["bank_free_S"][0] = sev
                S_["bank_free_S"][1] = sev
                p.wait("gpsimd", "S", sev)
                p.emit("gpsimd", lambda g: g.tensor_tensor(out=gq[:], in0=s2sb[:], in1=rbB[:], op=ALU.subtract), [("G", 1)])
                gv = C["G"]
                p.wait("scalar", "G", gv)
                p.emit("scalar", lambda s_: s_.activation(out=rbB[:], in_=gq[:], func=ACT.Ln, bias=col(OF_EPS)))
                p.emit("scalar", lambda s_: s_.activation(out=rbB[:], in_=rbB[:], func=ACT.Exp, scale=-0.5), [("S", 1)])
                state[("rb", sb)] = C["S"]
            th(ffn_ln)

            def ffn_apply_half(sb, h):
                for m in range(h * 4, h * 4 + 4):
                    p.wait("gpsimd", "S", state[("rb", sb)])
                    if m >= 2:
                        p.wait("gpsimd", "S", state[("gelu_m", sb, m - 2)])
                    p.emit("gpsimd", lambda g, mm=m: g.tensor_tensor(out=gvro[:, mm % 2, :], in0=y18[:, mm, :], in1=mub[:], op=ALU.subtract))
                    p.emit("gpsimd", lambda g, mm=m: g.tensor_tensor(out=gvro[:, mm % 2, :], in0=gvro[:, mm % 2, :], in1=rbB[:], op=ALU.mult), [("G", 1)])
                    gv = C["G"]
                    p.wait("scalar", "G", gv)
                    p.emit("scalar", lambda s_, mm=m, sb_=sb: s_.activation(
                        out=gbf[:, mm, :], in_=gvro[:, mm % 2, :], func=ACT.Gelu,
                        scale=col(OF_G + sb_ * 8 + mm), bias=col(OF_B + sb_ * 8 + mm)), [("S", 1)])
                    state[("gelu_m", sb, m)] = C["S"]
                if h == 1:
                    state[("gelu", sb)] = C["S"]
            th(lambda sb=sblk: ffn_apply_half(sb, 0))
            th(lambda sb=sblk: ffn_apply_half(sb, 1))

            linear(l2, gbf, 1, "resid", sblk, upd_h=(sblk < 2))

        def fin():
            g_last = C["G"]
            p.wait("sync", "G", g_last)
            p.wait("sync", "V", v_ln2_c)
            p.emit("sync", lambda s_, cc=c: s_.dma_start(out=o_r[:, :, cc * TC:(cc + 1) * TC], in_=x8[:, par:par + DH, :]), [("sout", 16)])
            S_["t_blast"] = C["T"]
            S_["g_blast"] = C["G"]
        th(fin)
        return thunks

    pending = []

    def pump(k=2):
        for _ in range(k):
            if pending:
                pending.pop(0)()
            elif dq_pending:
                dq_pending.pop(0)()
        for _ in range(2):
            if dq_pending:
                dq_pending.pop(0)()

    def flush():
        while dq_pending:
            dq_pending.pop(0)()
        while pending:
            pending.pop(0)()

    for c in range(nch):
        if c > 0:
            p.wait("sync", "sout", 16 * (c - 1))
            p.emit("sync", lambda s_, cc=c: s_.dma_start(
                out=x8[:, (cc % 2) * DH:(cc % 2) * DH + DH, :], in_=x_r[:, :, cc * TC:(cc + 1) * TC]), [("xin", 16)])
        emit_A(c, pump, flush)
        pending.extend(make_B(c))
    flush()

    p.wait("sync", "sout", 16 * nch)

    with nc.Block() as block:
        def mk(engname):
            def run(eng):
                for item in p.q[engname]:
                    if item[0] == "w":
                        eng.wait_ge(sem[item[1]], item[2])
                        if engname == "tensor":
                            for _ in range(8):
                                eng.nop()
                    else:
                        ins = item[1](eng)
                        for s, a in item[2]:
                            ins.then_inc(sem[s], a)
            return run
        block.sync(mk("sync"))
        block.scalar(mk("scalar"))
        block.vector(mk("vector"))
        block.tensor(mk("tensor"))
        block.gpsimd(mk("gpsimd"))

    st.close()
    return nc


def _stage(inputs):
    import ml_dtypes
    x = np.asarray(inputs["L0_attn_x"], dtype=np.float32)
    W_in = np.ascontiguousarray(np.asarray(inputs["L0_attn_W_in"], dtype=np.float32))
    conv_w = np.asarray(inputs["L0_attn_conv_w"], dtype=np.float32).reshape(D, 4)
    conv_b = np.asarray(inputs["L0_attn_conv_b"], dtype=np.float32)
    W_xp = np.ascontiguousarray(np.asarray(inputs["L0_attn_W_xproj"], dtype=np.float32))
    log_A = np.asarray(inputs["L0_attn_log_A"], dtype=np.float32)
    D_param = np.asarray(inputs["L0_attn_D_param"], dtype=np.float32)
    W_out = np.ascontiguousarray(np.asarray(inputs["L0_attn_W_out"], dtype=np.float32))
    q = np.asarray(inputs["L0_ffn_qweights"], dtype=np.int32)
    scales = np.asarray(inputs["L0_ffn_scales"], dtype=np.float32)
    biases = np.asarray(inputs["L0_ffn_biases"], dtype=np.float32)
    la = np.asarray(inputs["L0_ffn_lora_a"], dtype=np.float32)
    lb = np.asarray(inputs["L0_ffn_lora_b"], dtype=np.float32)
    g = np.asarray(inputs["L0_ffn_ln_gamma"], dtype=np.float32)
    b = np.asarray(inputs["L0_ffn_ln_beta"], dtype=np.float32)

    cols = np.zeros((128, COLS_W), dtype=np.float32)
    cols[:, OF_CONV:OF_CONV + 32] = conv_w.reshape(DH, 128, 4).transpose(1, 0, 2).reshape(128, 32)
    cols[:, OF_CB:OF_CB + 8] = conv_b.reshape(DH, 128).T
    cols[:, OF_D:OF_D + 8] = D_param.reshape(DH, 128).T
    cols[:, OF_BIAS:OF_BIAS + 48] = biases.reshape(6, DH, 128).transpose(2, 0, 1).reshape(128, 48)
    cols[:, OF_G:OF_G + 24] = g.reshape(3, DH, 128).transpose(2, 0, 1).reshape(128, 24)
    cols[:, OF_B:OF_B + 24] = b.reshape(3, DH, 128).transpose(2, 0, 1).reshape(128, 24)
    cols[:, OF_ONE] = 1.0
    cols[:, OF_EPS] = 1e-5
    cols[:, OF_LOGA:OF_LOGA + 128] = log_A.reshape(DH, 128, NST).transpose(1, 0, 2).reshape(128, 128)

    sel = np.zeros((128, 16 * 128), dtype=np.float32)
    for n in range(NST):
        sel[n, n * 128:(n + 1) * 128] = 1.0
    sel = sel.astype(ml_dtypes.bfloat16)
    zf32 = np.zeros((128, 1024), dtype=np.float32)
    zbf = np.zeros((128, 1024), dtype=np.float32).astype(ml_dtypes.bfloat16)
    zone_np = np.zeros((128, 128), dtype=np.float32)
    zone_np[0, :] = 1.0
    onesb_np = np.full((128, 128), 1.0 / 1024.0, dtype=np.float32).astype(ml_dtypes.bfloat16)

    qT = np.ascontiguousarray(q.transpose(0, 2, 1))
    sT = np.ascontiguousarray(np.repeat(scales, 16, axis=2).transpose(0, 2, 1))
    laT = np.ascontiguousarray(la.transpose(0, 2, 1))
    lbT = np.ascontiguousarray(lb.transpose(0, 2, 1))

    shared = dict(w_in=W_in, w_xp=W_xp, w_out=W_out, cols=cols, selB=sel,
                  zf32=zf32, zbf=zbf, zone_in=zone_np, onesb_in=onesb_np,
                  qT=qT, sT=sT, laT=laT, lbT=lbT)
    in_maps = []
    for bidx in range(8):
        m = dict(shared)
        m["x_fm"] = np.ascontiguousarray(x[bidx].T)
        in_maps.append(m)
    return in_maps


def kernel(**inputs):
    in_maps = _stage(inputs)
    nc = build_program()
    res = run_bass_kernel_spmd(nc, in_maps, core_ids=list(range(8)))
    out = np.stack([np.asarray(r["out_fm"]).T for r in res.results], axis=0)
    return np.ascontiguousarray(out.astype(np.float32))


# version 24
# speedup vs baseline: 1.1071x; 1.0040x over previous
import numpy as np
from contextlib import ExitStack

import concourse.bass as bass
import concourse.mybir as mybir
from concourse.bass_utils import run_bass_kernel_spmd

F32 = mybir.dt.float32
BF16 = mybir.dt.bfloat16
I32 = mybir.dt.int32
ALU = mybir.AluOpType
ACT = mybir.ActivationFunctionType

T = 4096
D = 1024
DH = 8
NST = 16
TC = 512
NCH = T // TC
R = 32

ENGINES = ("sync", "scalar", "vector", "tensor", "gpsimd")
SEMS = ("xin", "wcol", "wst", "sout", "T", "S", "V", "G",
        "wsA", "wsB", "wq0", "wq1", "dqA", "dqB", "wgp", "wiA", "wiB")

OF_CONV = 0
OF_CB = 32
OF_D = 40
OF_BIAS = 48
OF_G = 96
OF_B = 120
OF_ONE = 144
OF_EPS = 145
OF_LOGA = 152
COLS_W = 280


class Plan:
    def __init__(self):
        self.q = {e: [] for e in ENGINES}
        self.cnt = {s: 0 for s in SEMS}
        self.maxw = {e: {s: 0 for s in SEMS} for e in ENGINES}

    def wait(self, e, sem, val):
        if val <= 0 or val <= self.maxw[e][sem]:
            return
        self.maxw[e][sem] = val
        self.q[e].append(("w", sem, val))

    def emit(self, e, fn, incs=()):
        self.q[e].append(("i", fn, incs))
        for s, a in incs:
            self.cnt[s] += a


def build_program(nch=NCH):
    nc = bass.Bass("TRN2", target_bir_lowering=False, detect_race_conditions=False)

    x_in = nc.dram_tensor("x_fm", [D, T], F32, kind="ExternalInput").ap()
    w_in = nc.dram_tensor("w_in", [D, 2 * D], F32, kind="ExternalInput").ap()
    w_xp = nc.dram_tensor("w_xp", [D, 33], F32, kind="ExternalInput").ap()
    w_out = nc.dram_tensor("w_out", [D, D], F32, kind="ExternalInput").ap()
    cols_in = nc.dram_tensor("cols", [128, COLS_W], F32, kind="ExternalInput").ap()
    sel_in = nc.dram_tensor("selB", [128, 16 * 128], BF16, kind="ExternalInput").ap()
    zf32_in = nc.dram_tensor("zf32", [128, 1024], F32, kind="ExternalInput").ap()
    zbf_in = nc.dram_tensor("zbf", [128, 1024], BF16, kind="ExternalInput").ap()
    zone_in = nc.dram_tensor("zone_in", [128, 128], F32, kind="ExternalInput").ap()
    onesb_in = nc.dram_tensor("onesb_in", [128, 128], BF16, kind="ExternalInput").ap()
    qT_in = nc.dram_tensor("qT", [6, D, D], I32, kind="ExternalInput").ap()
    sT_in = nc.dram_tensor("sT", [6, D, D], F32, kind="ExternalInput").ap()
    laT_in = nc.dram_tensor("laT", [6, D, R], F32, kind="ExternalInput").ap()
    lbT_in = nc.dram_tensor("lbT", [6, R, D], F32, kind="ExternalInput").ap()
    out_fm = nc.dram_tensor("out_fm", [D, T], F32, kind="ExternalOutput").ap()
    wffn_sc = nc.dram_tensor("wffn_sc", [6, 2, DH, 128, TC], BF16, kind="Internal").ap()
    wout_sc = nc.dram_tensor("wout_sc", [DH, 128, D], BF16, kind="Internal").ap()
    win_sc = nc.dram_tensor("win_sc", [4, DH, 128, TC], BF16, kind="Internal").ap()
    lbT_sc = nc.dram_tensor("lbT_sc", [6, R, D], BF16, kind="Internal").ap()

    st = ExitStack()
    en = st.enter_context

    def sbuf(name, shape, dt):
        return en(nc.sbuf_tensor(name, shape, dt))

    cols = sbuf("colsb", [128, COLS_W], F32)
    lam = sbuf("lam", [128, 128], F32)
    selB = sbuf("selBb", [128, 16 * 128], BF16)
    zone = sbuf("zone", [128, 128], F32)
    onesb = sbuf("onesb", [128, 128], BF16)
    wx_bf = sbuf("wx_bf", [128, DH, 33], BF16)
    la_bf = sbuf("la_bf", [128, 6 * DH * R], BF16)
    lb_bf = sbuf("lb_bf", [128, 2, D], BF16)
    winr = sbuf("winr", [128, 4, TC], BF16)
    wk = sbuf("wk", [128, 4, TC], BF16)
    qstg = sbuf("qstg", [128, 1, TC], I32)
    sstg = sbuf("sstg", [128, 1, TC], F32)
    gq = sbuf("gq", [128, TC], F32)
    dqo = sbuf("dqo", [128, 1, TC], BF16)

    x8 = sbuf("x8", [128, 2 * DH, TC], F32)
    xmix = sbuf("xmix", [128, DH, TC], BF16)
    xhat = sbuf("xhat", [128, DH, TC], BF16)
    cast2 = sbuf("cast2", [128, 4, TC], BF16)
    sq2 = sbuf("sq2", [128, 4, TC], BF16)
    trot = sbuf("trot", [128, 2, TC], F32)
    xm = sbuf("xm", [128, 2, 3 + TC], F32)
    hist = sbuf("hist", [128, 24], F32)
    vrot = sbuf("vrot", [128, 2, TC], F32)
    gz = sbuf("gz", [128, DH, TC], BF16)
    u8 = sbuf("u8", [128, DH, TC], F32)
    ubf = sbuf("ubf", [128, 2, TC], BF16)
    Xb = sbuf("Xb", [128, TC], BF16)
    Xc = sbuf("Xc", [128, TC], BF16)
    dsb = sbuf("dsb", [128, TC], F32)
    dbsb = sbuf("dbsb", [128, TC], F32)
    bbn = sbuf("bbn", [128, 2, TC], F32)
    arot = sbuf("arot", [128, 3, TC], F32)
    brot = sbuf("brot", [128, 3, TC], F32)
    hrot = sbuf("hrot", [128, 3, TC], BF16)
    cbn = sbuf("cbn", [128, 4, TC], BF16)
    carry = sbuf("carry", [128, 128], F32)
    yacc = sbuf("yacc", [128, DH, TC], F32)
    ygbf = sbuf("ygbf", [128, DH, TC], BF16)
    rbsb = sbuf("rbsb", [128, TC], F32)
    y18 = sbuf("y18", [128, DH, TC], BF16)
    gbf = sbuf("gbf", [128, DH, TC], BF16)
    xabf = sbuf("xabf", [128, TC], BF16)
    sq2B = sbuf("sq2B", [128, 2, TC], BF16)
    mub = sbuf("mub", [128, TC], F32)
    s2sb = sbuf("s2sb", [128, TC], F32)
    rbB = sbuf("rbB", [128, TC], F32)
    gvro = sbuf("gvro", [128, 2, TC], F32)

    pb = [en(nc.psum_tensor(f"pb{i}", [128, TC], F32)) for i in range(8)]

    sem = {s: en(nc.semaphore(s)) for s in SEMS}

    p = Plan()
    C = p.cnt

    def col(off):
        return cols[:, off:off + 1]

    x_r = x_in.rearrange("(dh p) t -> p dh t", p=128)
    o_r = out_fm.rearrange("(dh p) t -> p dh t", p=128)

    for dst, srcap in [
        (lambda: cols[:], cols_in), (lambda: selB[:], sel_in),
        (lambda: zone[:], zone_in), (lambda: onesb[:], onesb_in),
        (lambda: carry[:], zf32_in[:, 0:128]), (lambda: hist[:], zf32_in[:, 0:24]),
        (lambda: dsb[:], zf32_in[:, 0:TC]), (lambda: Xb[:], zbf_in[:, 0:TC]),
        (lambda: Xc[:], zbf_in[:, 0:TC]), (lambda: xabf[:], zbf_in[:, 0:TC]),
        (lambda: lb_bf[:, 0, :], zbf_in), (lambda: lb_bf[:, 1, :], zbf_in),
    ]:
        p.emit("sync", lambda s_, d=dst, sa=srcap: s_.dma_start(out=d(), in_=sa), [("wcol", 16)])
    p.emit("sync", lambda s_: s_.dma_start(out=x8[:, 0:DH, :], in_=x_r[:, :, 0:TC]), [("xin", 16)])
    WCOL = C["wcol"]
    p.wait("scalar", "wcol", WCOL)
    p.emit("scalar", lambda s_: s_.activation(out=lam[:], in_=cols[:, OF_LOGA:OF_LOGA + 128], func=ACT.Exp), [("S", 1)])
    p.wait("vector", "S", C["S"])
    p.emit("vector", lambda v: v.tensor_scalar(out=lam[:], in0=lam[:], scalar1=-1.0, scalar2=None, op0=ALU.mult), [("V", 1)])

    w_in_r = w_in.rearrange("(dh p) m -> dh p m", p=128)
    w_xp_r = w_xp.rearrange("(dh p) m -> p dh m", p=128)
    la_r = laT_in.rearrange("l (kt p) r -> p l kt r", p=128)
    la_bf_v = la_bf[:].rearrange("p (l kt r) -> p l kt r", l=6, kt=DH)
    w_out_r = w_out.rearrange("(dh p) m -> dh p m", p=128)
    for q in range(4):
        p.emit("gpsimd", lambda g, qq=q: g.dma_start(
            out=win_sc[qq], in_=w_in_r[:, :, qq * TC:(qq + 1) * TC]), [("wgp", 16)])
    p.emit("gpsimd", lambda g: g.dma_start(out=wx_bf[:], in_=w_xp_r), [("wgp", 16)])
    p.emit("gpsimd", lambda g: g.dma_start(out=la_bf_v, in_=la_r), [("wgp", 16)])
    p.emit("gpsimd", lambda g: g.dma_start(out=wout_sc[:], in_=w_out_r), [("wgp", 16)])
    p.emit("gpsimd", lambda g: g.dma_start(out=lbT_sc[:], in_=lbT_in), [("wgp", 16)])
    WGP_ALL = C["wgp"]

    dq_pending = []
    dq_lin_done = []
    dq_state = {"npc": 0, "g": {}, "st": []}

    def mk_dq(l, kt, hf):
        def run():
            npc = dq_state["npc"]
            dqs = "dqA" if npc % 2 == 0 else "dqB"
            if npc >= 1:
                p.wait("sync", "G", dq_state["g"][npc - 1])
            p.emit("sync", lambda s_: s_.dma_start(
                out=qstg[:, 0, :], in_=qT_in[l, kt * 128:(kt + 1) * 128, hf * TC:(hf + 1) * TC]), [(dqs, 16)])
            p.emit("sync", lambda s_: s_.dma_start(
                out=sstg[:, 0, :], in_=sT_in[l, kt * 128:(kt + 1) * 128, hf * TC:(hf + 1) * TC]), [(dqs, 16)])
            p.wait("gpsimd", dqs, C[dqs])
            if npc >= 1:
                p.wait("gpsimd", dq_state["st"][npc - 1][0], dq_state["st"][npc - 1][1])
            p.emit("gpsimd", lambda g: g.tensor_scalar(
                out=gq[:], in0=qstg[:, 0, :], scalar1=1.0 / 7.5, scalar2=-1.0, op0=ALU.mult, op1=ALU.add))
            p.emit("gpsimd", lambda g: g.tensor_tensor(
                out=dqo[:, 0, :], in0=gq[:], in1=sstg[:, 0, :], op=ALU.mult), [("G", 1)])
            dq_state["g"][npc] = C["G"]
            p.wait("sync", "G", C["G"])
            wqs = "wq0" if npc % 2 == 0 else "wq1"
            p.emit("sync", lambda s_: s_.dma_start(out=wffn_sc[l, hf, kt], in_=dqo[:, 0, :]), [(wqs, 16)])
            dq_state["st"].append((wqs, C[wqs]))
            dq_state["npc"] = npc + 1
        return run

    nq0 = nq1 = 0
    for l in range(6):
        for kt in range(DH):
            for hf in range(2):
                dq_pending.append(mk_dq(l, kt, hf))
                if (l * 16 + kt * 2 + hf) % 2 == 0:
                    nq0 += 1
                else:
                    nq1 += 1
        dq_lin_done.append([("wq0", 16 * nq0), ("wq1", 16 * nq1)])
    G_DQ_END = 0

    S_ = {
        "v_done127": 0, "g_done127": 0,
        "tbb15": 0, "tcb15": 0, "sbb15": 0, "scb15": 0,
        "t_dltab": 0, "vgate": 0, "vconv7": 0, "v_ln2": 0,
        "s_xcevac": 0, "t_wout": 0, "t_blast": 0, "g_blast": 0,
        "bank_free_S": [0] * 8, "bank_free_V": [0] * 8,
        "s_carry_all": 0, "g_b_last": G_DQ_END, "v_yinit": 0,
        "t_lb": 0, "s_xhat": 0,
    }

    def stats_pair(src_getter, bank_mu, bank_s2, bf_direct=False):
        s0 = C["S"]
        tb0 = C["T"]
        for kt in range(DH):
            if bf_direct:
                if kt >= 2:
                    p.wait("scalar", "T", tb0 + kt - 1)
            elif kt >= 4:
                p.wait("scalar", "T", tb0 + kt - 3)
            if not bf_direct:
                p.emit("scalar", lambda s_, k=kt: s_.copy(out=cast2[:, k % 4, :], in_=src_getter(k)))
                p.emit("scalar", lambda s_, k=kt: s_.activation(out=sq2[:, k % 4, :], in_=src_getter(k), func=ACT.Square), [("S", 1)])
                p.wait("tensor", "S", s0 + kt + 1)
                p.emit("tensor", lambda t_, k=kt, b=bank_mu: t_.matmul(out=pb[b][:], lhsT=onesb[:], rhs=cast2[:, k % 4, :], start=(k == 0), stop=(k == DH - 1)))
                p.emit("tensor", lambda t_, k=kt, b=bank_s2: t_.matmul(out=pb[b][:], lhsT=onesb[:], rhs=sq2[:, k % 4, :], start=(k == 0), stop=(k == DH - 1)), [("T", 1)])
            else:
                p.emit("scalar", lambda s_, k=kt: s_.activation(out=sq2B[:, k % 2, :], in_=src_getter(k), func=ACT.Square), [("S", 1)])
                p.wait("tensor", "S", s0 + kt + 1)
                p.emit("tensor", lambda t_, k=kt, b=bank_mu: t_.matmul(out=pb[b][:], lhsT=onesb[:], rhs=src_getter(k), start=(k == 0), stop=(k == DH - 1)))
                p.emit("tensor", lambda t_, k=kt, b=bank_s2: t_.matmul(out=pb[b][:], lhsT=onesb[:], rhs=sq2B[:, k % 2, :], start=(k == 0), stop=(k == DH - 1)), [("T", 1)])
        return C["T"]

    def rb_from_stats(bank_mu, bank_s2, rb_dst):
        tstats = C["T"]
        p.wait("scalar", "T", tstats)
        p.emit("scalar", lambda s_, b=bank_mu: s_.activation(out=trot[:, 0, :], in_=pb[b][:], func=ACT.Square), [("S", 1)])
        p.wait("vector", "S", C["S"])
        p.wait("vector", "T", tstats)
        p.emit("vector", lambda v, b=bank_s2: v.tensor_tensor(out=vrot[:, 0, :], in0=pb[b][:], in1=trot[:, 0, :], op=ALU.subtract), [("V", 1)])
        p.wait("scalar", "V", C["V"])
        p.emit("scalar", lambda s_, r=rb_dst: s_.activation(out=r[:], in_=vrot[:, 0, :], func=ACT.Ln, bias=col(OF_EPS)))
        p.emit("scalar", lambda s_, r=rb_dst: s_.activation(out=r[:], in_=r[:], func=ACT.Exp, scale=-0.5), [("S", 1)])
        return C["S"]

    def emit_A(c, pump, flush):
        par = (c % 2) * DH

        def xs(dh):
            return x8[:, par + dh, :]

        for e in ("vector", "scalar", "tensor"):
            p.wait(e, "xin", 16 * (c + 1))
            p.wait(e, "wcol", WCOL)
        p.wait("tensor", "wgp", WGP_ALL)
        p.wait("tensor", "S", S_["s_xcevac"])
        p.wait("tensor", "S", S_["sbb15"])
        p.wait("tensor", "V", S_["v_ln2"])
        stats_pair(xs, 4, 5)
        srb = rb_from_stats(4, 5, rbsb)
        for kt in range(DH):
            p.wait("vector", "S", srb)
            p.emit("vector", lambda v, k=kt: v.tensor_tensor(out=trot[:, k % 2, :], in0=xs(k), in1=pb[4][:], op=ALU.subtract))
            p.emit("vector", lambda v, k=kt: v.tensor_tensor(out=xmix[:, k, :], in0=trot[:, k % 2, :], in1=rbsb[:], op=ALU.mult), [("V", 1)])
        v_ln1 = C["V"]
        pump()

        sevac = {}
        vconv = {}
        ssilu = {}
        for q in range(4):
            wis = "wiA" if q % 2 == 0 else "wiB"
            tq0 = C["T"]
            wivals = []
            for kt in range(DH):
                if q == 0 and kt < 4:
                    p.wait("sync", "T", S_["t_wout"])
                p.wait("sync", "T", tq0 + kt - 3)
                p.emit("sync", lambda s_, qq=q, k=kt: s_.dma_start(out=winr[:, k % 4, :], in_=win_sc[qq, k]), [(wis, 16)])
                wivals.append((wis, C[wis]))
            p.wait("tensor", "V", v_ln1)
            for kt in range(DH):
                p.wait("tensor", wivals[kt][0], wivals[kt][1])
                for j in range(4):
                    bank = 4 + j
                    if kt == 0:
                        p.wait("tensor", "S", S_["bank_free_S"][bank])
                    p.emit("tensor", lambda t_, k=kt, jj=j, b=bank: t_.matmul(
                        out=pb[b][:], lhsT=winr[:, k % 4, jj * 128:(jj + 1) * 128], rhs=xmix[:, k, :],
                        start=(k == 0), stop=(k == DH - 1)),
                        [("T", 1)] if j == 3 else ())
            tq_done = C["T"]
            for j in range(4):
                m = q * 4 + j
                bank = 4 + j
                p.wait("scalar", "T", tq_done)
                if m < 8:
                    dh = m
                    sl = m % 2
                    if m >= 2:
                        p.wait("scalar", "V", vconv[m - 2])
                    else:
                        p.wait("scalar", "V", S_["vconv7"])
                    p.emit("scalar", lambda s_, d=dh, s3=sl: s_.copy(out=xm[:, s3, 0:3], in_=hist[:, d * 3:(d + 1) * 3]))
                    p.emit("scalar", lambda s_, d=dh, s3=sl, b=bank: s_.copy(out=xm[:, s3, 3:3 + TC], in_=pb[b][:]), [("S", 1)])
                    sevac[m] = C["S"]
                    S_["bank_free_S"][bank] = C["S"]
                    p.wait("vector", "S", sevac[m])
                    if m >= 2:
                        p.wait("vector", "S", ssilu.get(m - 2, 0))
                    p.emit("vector", lambda v, d=dh, s3=sl: v.tensor_scalar(
                        out=vrot[:, d % 2, :], in0=xm[:, s3, 0:TC], scalar1=col(OF_CONV + d * 4 + 0),
                        scalar2=col(OF_CB + d), op0=ALU.mult, op1=ALU.add))
                    for k in range(1, 4):
                        p.emit("vector", lambda v, d=dh, s3=sl, kk=k: v.scalar_tensor_tensor(
                            out=vrot[:, d % 2, :], in0=xm[:, s3, kk:kk + TC], scalar=col(OF_CONV + d * 4 + kk),
                            in1=vrot[:, d % 2, :], op0=ALU.mult, op1=ALU.add), [("V", 1)] if k == 3 else ())
                    vconv[m] = C["V"]
                    p.wait("scalar", "V", vconv[m])
                    p.emit("scalar", lambda s_, d=dh, s3=sl: s_.copy(out=hist[:, d * 3:(d + 1) * 3], in_=xm[:, s3, TC:TC + 3]))
                    if m == 0:
                        p.wait("scalar", "G", S_["g_done127"])
                        p.wait("scalar", "V", S_["v_yinit"])
                    p.emit("scalar", lambda s_, d=dh: s_.activation(out=u8[:, d, :], in_=vrot[:, d % 2, :], func=ACT.Silu), [("S", 1)])
                    ssilu[m] = C["S"]
                else:
                    dh = m - 8
                    if m == 8:
                        p.wait("scalar", "V", S_["vgate"])
                    p.emit("scalar", lambda s_, d=dh, b=bank: s_.activation(out=gz[:, d, :], in_=pb[b][:], func=ACT.Silu), [("S", 1)])
                    sevac[m] = C["S"]
                    S_["bank_free_S"][bank] = C["S"]
            pump()
        S_["vconv7"] = vconv[7]

        txp0 = C["T"]
        for kt in range(DH):
            if kt >= 2:
                p.wait("scalar", "T", txp0 + kt - 1)
            p.emit("scalar", lambda s_, k=kt: s_.copy(out=ubf[:, k % 2, :], in_=u8[:, k, :]), [("S", 1)])
            p.wait("tensor", "S", C["S"])
            if kt == 0:
                p.wait("tensor", "S", S_["s_xcevac"])
                p.wait("tensor", "S", S_["scb15"])
            p.emit("tensor", lambda t_, k=kt: t_.matmul(out=pb[4][0:16, :], lhsT=wx_bf[:, k, 0:16], rhs=ubf[:, k % 2, :],
                                                        start=(k == 0), stop=(k == DH - 1)))
            p.emit("tensor", lambda t_, k=kt: t_.matmul(out=pb[5][0:16, :], lhsT=wx_bf[:, k, 16:32], rhs=ubf[:, k % 2, :],
                                                        start=(k == 0), stop=(k == DH - 1)))
            p.emit("tensor", lambda t_, k=kt: t_.matmul(out=pb[6][0:1, :], lhsT=wx_bf[:, k, 32:33], rhs=ubf[:, k % 2, :],
                                                        start=(k == 0), stop=(k == DH - 1)), [("T", 1)])
        p.wait("scalar", "T", C["T"])
        p.wait("scalar", "T", S_["tcb15"])
        p.emit("scalar", lambda s_: s_.copy(out=Xc[0:16, :], in_=pb[5][0:16, :]))
        p.emit("scalar", lambda s_: s_.activation(out=rbsb[0:1, 0:TC], in_=pb[6][0:1, :], func=ACT.Abs))
        p.emit("scalar", lambda s_: s_.activation(out=rbsb[0:1, 0:TC], in_=rbsb[0:1, 0:TC], func=ACT.Exp, scale=-1.0))
        p.emit("scalar", lambda s_: s_.activation(out=rbsb[0:1, 0:TC], in_=rbsb[0:1, 0:TC], func=ACT.Ln, bias=cols[0:1, OF_ONE:OF_ONE + 1]))
        p.emit("scalar", lambda s_: s_.activation(out=dsb[0:1, 0:TC], in_=pb[6][0:1, :], func=ACT.Relu), [("S", 1)])
        S_["s_xcevac"] = C["S"]
        sxpe = C["S"]
        p.wait("vector", "S", sxpe)
        p.emit("vector", lambda v: v.tensor_tensor(out=dsb[0:1, 0:TC], in0=rbsb[0:1, 0:TC], in1=dsb[0:1, 0:TC], op=ALU.add), [("V", 1)])
        vdel = C["V"]
        p.wait("tensor", "V", vdel)
        p.wait("tensor", "S", S_["scb15"])
        p.emit("tensor", lambda t_: t_.matmul(out=pb[6][:], lhsT=zone[:], rhs=dsb[:, 0:TC], start=True, stop=True), [("T", 1)])
        S_["t_dltab"] = C["T"]
        p.wait("scalar", "T", C["T"])
        p.emit("scalar", lambda s_: s_.copy(out=dbsb[:], in_=pb[6][:]), [("S", 1)])
        sdb = C["S"]
        p.wait("vector", "S", sdb)
        p.wait("vector", "T", S_["tbb15"])
        p.emit("vector", lambda v: v.tensor_tensor(out=Xb[0:16, :], in0=pb[4][0:16, :], in1=dbsb[0:16, :], op=ALU.mult), [("V", 1)])
        vbp = C["V"]
        pump()

        for dh in range(DH):
            if dh == 0:
                p.wait("vector", "G", S_["g_done127"])
            p.emit("vector", lambda v, d=dh: v.tensor_scalar(
                out=yacc[:, d, :], in0=u8[:, d, :], scalar1=col(OF_D + d), scalar2=None, op0=ALU.mult),
                [("V", 1)] if dh == DH - 1 else ())
        S_["v_yinit"] = C["V"]
        v_done = {}
        g_done = {}
        sa_done = {}
        scarry = {}
        sbb = {}
        scb = {}
        tbb = {}
        tcb = {}
        prev_vd127 = S_["v_done127"]
        prev_sbb15 = S_["sbb15"]
        prev_scb15 = S_["scb15"]
        prev_gb = S_["g_b_last"]
        def emit_bbcb_mm(n):
            p.wait("tensor", "S", sbb[n - 1] if n >= 1 else prev_sbb15)
            p.wait("tensor", "V", vbp)
            p.emit("tensor", lambda t_, nn=n: t_.matmul(out=pb[5][:], lhsT=selB[:, nn * 128:(nn + 1) * 128], rhs=Xb[:], start=True, stop=True), [("T", 1)])
            tbb[n] = C["T"]
            p.wait("tensor", "S", scb[n - 2] if n >= 2 else prev_scb15)
            p.emit("tensor", lambda t_, nn=n: t_.matmul(out=pb[6 + nn % 2][:], lhsT=selB[:, nn * 128:(nn + 1) * 128], rhs=Xc[:], start=True, stop=True), [("T", 1)])
            tcb[n] = C["T"]

        emit_bbcb_mm(0)
        for n in range(NST):
            p.wait("scalar", "T", tbb[n])
            p.wait("scalar", "G", g_done[(n - 1) * 8 - 1] if n >= 2 else prev_gb)
            p.emit("scalar", lambda s_, nn=n: s_.copy(out=bbn[:, nn % 2, :], in_=pb[5][:]), [("S", 1)])
            sbb[n] = C["S"]
            p.wait("scalar", "T", tcb[n])
            p.wait("scalar", "V", v_done[(n - 3) * 8 - 1] if n >= 4 else prev_vd127)
            p.emit("scalar", lambda s_, nn=n: s_.copy(out=cbn[:, nn % 4, :], in_=pb[6 + nn % 2][:]), [("S", 1)])
            scb[n] = C["S"]
            if n + 1 < NST:
                emit_bbcb_mm(n + 1)
            for dh in range(DH):
                i = n * 8 + dh
                if i >= 3:
                    p.wait("scalar", "V", v_done[i - 3])
                    p.emit("scalar", lambda s_, ii=i - 3: s_.copy(out=carry[:, ii:ii + 1], in_=hrot[:, ii % 3, TC - 1:TC]), [("S", 1)])
                    scarry[i - 3] = C["S"]
                p.emit("scalar", lambda s_, nn=n, d=dh, sl=i % 3: s_.activation(
                    out=arot[:, sl, :], in_=dbsb[:], func=ACT.Exp, scale=lam[:, d * 16 + nn:d * 16 + nn + 1]), [("S", 1)])
                sa_done[i] = C["S"]
                p.wait("gpsimd", "S", sbb[n])
                p.wait("gpsimd", "V", v_done[i - 3] if i >= 3 else prev_vd127)
                p.emit("gpsimd", lambda g, nn=n, d=dh, sl=i % 3: g.tensor_tensor(
                    out=brot[:, sl, :], in0=u8[:, d, :], in1=bbn[:, nn % 2, :], op=ALU.mult), [("G", 1)])
                g_done[i] = C["G"]
                p.wait("vector", "S", sa_done[i])
                p.wait("vector", "G", g_done[i])
                if i >= 3:
                    p.wait("vector", "S", scarry[i - 3])
                else:
                    p.wait("vector", "S", S_["s_carry_all"])
                p.emit("vector", lambda v, sl=i % 3, ii=i: v.tensor_tensor_scan(
                    out=hrot[:, sl, :], data0=arot[:, sl, :], data1=brot[:, sl, :],
                    initial=carry[:, ii:ii + 1], op0=ALU.mult, op1=ALU.add))
                p.wait("vector", "S", scb[n])
                p.emit("vector", lambda v, sl=i % 3, nn=n: v.tensor_tensor(
                    out=cast2[:, 0, :], in0=hrot[:, sl, :], in1=cbn[:, nn % 4, :], op=ALU.mult))
                p.emit("vector", lambda v, d=dh: v.tensor_tensor(
                    out=yacc[:, d, :], in0=yacc[:, d, :], in1=cast2[:, 0, :], op=ALU.add), [("V", 1)])
                v_done[i] = C["V"]
            pump()
        for ii in range(NST * 8 - 3, NST * 8):
            p.wait("scalar", "V", v_done[ii])
            p.emit("scalar", lambda s_, i2=ii: s_.copy(out=carry[:, i2:i2 + 1], in_=hrot[:, i2 % 3, TC - 1:TC]), [("S", 1)])
        S_["s_carry_all"] = C["S"]
        S_["v_done127"] = v_done[NST * 8 - 1]
        S_["g_done127"] = g_done[NST * 8 - 1]
        S_["g_b_last"] = g_done[NST * 8 - 1]
        S_["tbb15"] = tbb[NST - 1]
        S_["tcb15"] = tcb[NST - 1]
        S_["sbb15"] = sbb[NST - 1]
        S_["scb15"] = scb[NST - 1]
        S_["bank_free_S"][5] = sbb[NST - 1]
        S_["bank_free_S"][6] = max(S_["bank_free_S"][6], scb[NST - 2])
        S_["bank_free_S"][7] = max(S_["bank_free_S"][7], scb[NST - 1])

        flush()

        for dh in range(DH):
            p.emit("vector", lambda v, d=dh: v.tensor_tensor(out=ygbf[:, d, :], in0=yacc[:, d, :], in1=gz[:, d, :], op=ALU.mult),
                   [("V", 1)] if dh == DH - 1 else ())
        S_["vgate"] = C["V"]

        tH0 = C["T"]
        p.wait("sync", "wgp", WGP_ALL)
        p.wait("sync", "T", S_["t_blast"])
        wkt = []
        for kt in range(DH):
            p.wait("sync", "T", tH0 + kt - 3)
            wqs = "wsA" if kt % 2 == 0 else "wsB"
            p.emit("sync", lambda s_, k=kt: s_.dma_start(out=wk[:, k % 4, 0:TC], in_=wout_sc[k][:, 0:TC]), [(wqs, 16)])
            p.emit("sync", lambda s_, k=kt: s_.dma_start(out=winr[:, k % 4, 0:TC], in_=wout_sc[k][:, TC:2 * TC]), [(wqs, 16)])
            wkt.append((wqs, C[wqs]))
        p.wait("tensor", "V", S_["vgate"])
        for b in range(8):
            p.wait("tensor", "S", S_["bank_free_S"][b])
            p.wait("tensor", "V", S_["bank_free_V"][b])
        for kt in range(DH):
            p.wait("tensor", wkt[kt][0], wkt[kt][1])
            for m in range(DH):
                src = wk if m < 4 else winr
                mm_l = m if m < 4 else m - 4
                p.emit("tensor", lambda t_, k=kt, mm=m, s2=src, ml=mm_l: t_.matmul(
                    out=pb[mm][:], lhsT=s2[:, k % 4, ml * 128:(ml + 1) * 128], rhs=ygbf[:, k, :],
                    start=(k == 0), stop=(k == DH - 1)),
                    [("T", 1)] if m == DH - 1 else ())
        tHdone = C["T"]
        S_["t_wout"] = tHdone
        p.wait("vector", "T", tHdone)
        for m in range(DH):
            p.emit("vector", lambda v, mm=m: v.tensor_tensor(out=xs(mm), in0=xs(mm), in1=pb[mm][:], op=ALU.add),
                   [("V", 1)] if m == DH - 1 else ())
        vresid = C["V"]
        for m in range(DH):
            S_["bank_free_V"][m] = vresid

        p.wait("tensor", "G", S_["g_blast"])
        p.wait("tensor", "T", S_["t_blast"])
        p.wait("tensor", "V", vresid)
        p.wait("scalar", "V", vresid)
        stats_pair(xs, 4, 5)
        srb2 = rb_from_stats(4, 5, rbsb)
        sxh0 = C["S"]
        for kt in range(DH):
            p.wait("vector", "S", srb2 + max(0, kt - 1))
            p.emit("vector", lambda v, k=kt: v.tensor_tensor(out=trot[:, k % 2, :], in0=xs(k), in1=pb[4][:], op=ALU.subtract))
            p.emit("vector", lambda v, k=kt: v.tensor_tensor(out=vrot[:, k % 2, :], in0=trot[:, k % 2, :], in1=rbsb[:], op=ALU.mult))
            p.emit("vector", lambda v, k=kt: v.tensor_tensor(out=xs(k), in0=xs(k), in1=vrot[:, k % 2, :], op=ALU.add), [("V", 1)])
            p.wait("scalar", "V", C["V"])
            if kt == 0:
                p.wait("scalar", "G", S_["g_blast"])
            p.emit("scalar", lambda s_, k=kt: s_.copy(out=xhat[:, k, :], in_=vrot[:, k % 2, :]), [("S", 1)])
        bfree = C["S"]
        for b in (4, 5):
            S_["bank_free_S"][b] = bfree
        S_["v_ln2"] = C["V"]
        S_["s_xhat"] = bfree

    def make_B(c):
        par = (c % 2) * DH

        def xs(dh):
            return x8[:, par + dh, :]

        thunks = []

        def th(fn):
            thunks.append(fn)

        state = {}
        v_ln2_c = S_["v_ln2"]
        s_xhat_c = S_["s_xhat"]

        for sblk in range(3):
            l1, l2 = 2 * sblk, 2 * sblk + 1

            def lb_load(a=l1, b2=l2, sb=sblk):
                p.wait("sync", "wgp", WGP_ALL)
                p.wait("sync", "T", S_["t_lb"])
                p.emit("sync", lambda s_, li=a: s_.dma_start(out=lb_bf[0:R, 0, :], in_=lbT_sc[li]), [("wst", 16)])
                p.emit("sync", lambda s_, li=b2: s_.dma_start(out=lb_bf[0:R, 1, :], in_=lbT_sc[li]), [("wst", 16)])
                state[("lb", sb)] = C["wst"]
            th(lb_load)

            def linear(lin, src_bf, lb_slot, out_kind, sb, upd_h=False):
                def loraA(lin=lin, src_bf=src_bf, sb=sb):
                    p.wait("tensor", "S", S_["bank_free_S"][2])
                    p.wait("tensor", "S", s_xhat_c)
                    if src_bf is gbf:
                        p.wait("tensor", "S", state[("gelu", sb)])
                    p.emit("tensor", lambda t_: t_.nop())
                    for kt in range(DH):
                        off = (lin * 8 + kt) * R
                        p.emit("tensor", lambda t_, k=kt, o=off, sf=src_bf: t_.matmul(
                            out=pb[2][0:R, :], lhsT=la_bf[:, o:o + R], rhs=sf[:, k, :],
                            start=(k == 0), stop=(k == DH - 1)), [("T", 1)] if kt == DH - 1 else ())
                    state[("lorA", lin)] = C["T"]
                th(loraA)

                def xaev(lin=lin):
                    p.wait("scalar", "T", state[("lorA", lin)])
                    p.emit("scalar", lambda s_: s_.copy(out=xabf[0:R, :], in_=pb[2][0:R, :]), [("S", 1)])
                    state[("xa", lin)] = C["S"]
                    S_["bank_free_S"][2] = C["S"]
                th(xaev)

                for wv in range(2):
                    def wave(lin=lin, wv=wv, src_bf=src_bf, lb_slot=lb_slot, out_kind=out_kind, sb=sb, upd_h=upd_h):
                        t0w = C["T"]
                        for sname, sval in dq_lin_done[lin]:
                            p.wait("sync", sname, sval)
                        p.wait("sync", "T", S_["t_wout"])
                        wivals = []
                        for kt in range(DH):
                            p.wait("sync", "T", t0w + kt - 3)
                            wqs = "wsA" if kt % 2 == 0 else "wsB"
                            p.emit("sync", lambda s_, li=lin, k=kt, w=wv: s_.dma_start(
                                out=wk[:, k % 4, 0:TC], in_=wffn_sc[li, w, k]), [(wqs, 16)])
                            wivals.append((wqs, C[wqs]))
                        p.wait("tensor", "S", state[("xa", lin)])
                        p.wait("tensor", "S", s_xhat_c)
                        for kt in range(DH):
                            p.wait("tensor", wivals[kt][0], wivals[kt][1])
                            for j in range(4):
                                bank = j
                                if kt == 0:
                                    p.wait("tensor", "S", S_["bank_free_S"][bank])
                                p.emit("tensor", lambda t_, k=kt, jj=j, b=bank, sf=src_bf: t_.matmul(
                                    out=pb[b][:], lhsT=wk[:, k % 4, jj * 128:(jj + 1) * 128], rhs=sf[:, k, :],
                                    start=(k == 0), stop=False), [("T", 1)] if (j == 3) else ())
                        p.wait("tensor", "wst", state[("lb", sb)])
                        for j in range(4):
                            m = wv * 4 + j
                            bank = j
                            p.emit("tensor", lambda t_, b=bank, mm=m, ls=lb_slot: t_.matmul(
                                out=pb[b][:], lhsT=lb_bf[:, ls, mm * 128:(mm + 1) * 128], rhs=xabf[:],
                                start=False, stop=True), [("T", 1)])
                            tl = C["T"]
                            if out_kind == "resid" and wv == 1 and j == 3:
                                S_["t_lb"] = tl
                            p.wait("scalar", "T", tl)
                            if out_kind == "y1":
                                p.emit("scalar", lambda s_, b=bank, mm=m, li=lin: s_.activation(
                                    out=y18[:, mm, :], in_=pb[b][:], func=ACT.Identity, bias=col(OF_BIAS + li * 8 + mm)), [("S", 1)])
                                state[("y1", lin, m)] = C["S"]
                            else:
                                p.wait("scalar", "G", state.get(("res", sb, m - 2), 0))
                                p.emit("scalar", lambda s_, b=bank, mm=m, li=lin: s_.activation(
                                    out=gvro[:, mm % 2, :], in_=pb[b][:], func=ACT.Identity, bias=col(OF_BIAS + li * 8 + mm)), [("S", 1)])
                                sev = C["S"]
                                p.wait("gpsimd", "S", sev)
                                p.wait("gpsimd", "V", v_ln2_c)
                                if upd_h:
                                    p.emit("gpsimd", lambda g, mm=m: g.tensor_tensor(
                                        out=xs(mm), in0=xs(mm), in1=gvro[:, mm % 2, :], op=ALU.add))
                                    p.emit("gpsimd", lambda g, mm=m: g.tensor_tensor(
                                        out=xhat[:, mm, :], in0=xhat[:, mm, :], in1=gvro[:, mm % 2, :], op=ALU.add), [("G", 1)])
                                else:
                                    p.emit("gpsimd", lambda g, mm=m: g.tensor_tensor(
                                        out=xs(mm), in0=xs(mm), in1=gvro[:, mm % 2, :], op=ALU.add), [("G", 1)])
                                state[("res", sb, m)] = C["G"]
                            S_["bank_free_S"][bank] = C["S"]
                    th(wave)

            linear(l1, xhat, 0, "y1", sblk)

            def ffn_ln(sb=sblk, lin=l1):
                p.wait("tensor", "S", S_["bank_free_S"][0])
                p.wait("tensor", "S", S_["bank_free_S"][1])
                p.wait("tensor", "S", state[("y1", lin, 7)])
                tst = stats_pair(lambda k: y18[:, k, :], 0, 1, bf_direct=True)
                p.wait("scalar", "T", tst)
                p.emit("scalar", lambda s_: s_.copy(out=mub[:], in_=pb[0][:]))
                p.emit("scalar", lambda s_: s_.copy(out=s2sb[:], in_=pb[1][:]))
                p.emit("scalar", lambda s_: s_.activation(out=rbB[:], in_=pb[0][:], func=ACT.Square), [("S", 1)])
                sev = C["S"]
                S_["bank_free_S"][0] = sev
                S_["bank_free_S"][1] = sev
                p.wait("gpsimd", "S", sev)
                p.emit("gpsimd", lambda g: g.tensor_tensor(out=gq[:], in0=s2sb[:], in1=rbB[:], op=ALU.subtract), [("G", 1)])
                gv = C["G"]
                p.wait("scalar", "G", gv)
                p.emit("scalar", lambda s_: s_.activation(out=rbB[:], in_=gq[:], func=ACT.Ln, bias=col(OF_EPS)))
                p.emit("scalar", lambda s_: s_.activation(out=rbB[:], in_=rbB[:], func=ACT.Exp, scale=-0.5), [("S", 1)])
                state[("rb", sb)] = C["S"]
            th(ffn_ln)

            def ffn_apply_half(sb, h):
                for m in range(h * 4, h * 4 + 4):
                    p.wait("gpsimd", "S", state[("rb", sb)])
                    if m >= 2:
                        p.wait("gpsimd", "S", state[("gelu_m", sb, m - 2)])
                    p.emit("gpsimd", lambda g, mm=m: g.tensor_tensor(out=gvro[:, mm % 2, :], in0=y18[:, mm, :], in1=mub[:], op=ALU.subtract))
                    p.emit("gpsimd", lambda g, mm=m: g.tensor_tensor(out=gvro[:, mm % 2, :], in0=gvro[:, mm % 2, :], in1=rbB[:], op=ALU.mult), [("G", 1)])
                    gv = C["G"]
                    p.wait("scalar", "G", gv)
                    p.emit("scalar", lambda s_, mm=m, sb_=sb: s_.activation(
                        out=gbf[:, mm, :], in_=gvro[:, mm % 2, :], func=ACT.Gelu,
                        scale=col(OF_G + sb_ * 8 + mm), bias=col(OF_B + sb_ * 8 + mm)), [("S", 1)])
                    state[("gelu_m", sb, m)] = C["S"]
                if h == 1:
                    state[("gelu", sb)] = C["S"]
            th(lambda sb=sblk: ffn_apply_half(sb, 0))
            th(lambda sb=sblk: ffn_apply_half(sb, 1))

            linear(l2, gbf, 1, "resid", sblk, upd_h=(sblk < 2))

        def fin():
            g_last = C["G"]
            p.wait("sync", "G", g_last)
            p.wait("sync", "V", v_ln2_c)
            p.emit("sync", lambda s_, cc=c: s_.dma_start(out=o_r[:, :, cc * TC:(cc + 1) * TC], in_=x8[:, par:par + DH, :]), [("sout", 16)])
            S_["t_blast"] = C["T"]
            S_["g_blast"] = C["G"]
        th(fin)
        return thunks

    pending = []

    def pump(k=2):
        for _ in range(k):
            if pending:
                pending.pop(0)()
            elif dq_pending:
                dq_pending.pop(0)()
        for _ in range(2):
            if dq_pending:
                dq_pending.pop(0)()

    def flush():
        while dq_pending:
            dq_pending.pop(0)()
        while pending:
            pending.pop(0)()

    for c in range(nch):
        if c > 0:
            p.wait("sync", "sout", 16 * (c - 1))
            p.emit("sync", lambda s_, cc=c: s_.dma_start(
                out=x8[:, (cc % 2) * DH:(cc % 2) * DH + DH, :], in_=x_r[:, :, cc * TC:(cc + 1) * TC]), [("xin", 16)])
        emit_A(c, pump, flush)
        pending.extend(make_B(c))
    flush()

    p.wait("sync", "sout", 16 * nch)

    with nc.Block() as block:
        def mk(engname):
            def run(eng):
                for item in p.q[engname]:
                    if item[0] == "w":
                        eng.wait_ge(sem[item[1]], item[2])
                        if engname == "tensor":
                            for _ in range(8):
                                eng.nop()
                    else:
                        ins = item[1](eng)
                        for s, a in item[2]:
                            ins.then_inc(sem[s], a)
            return run
        block.sync(mk("sync"))
        block.scalar(mk("scalar"))
        block.vector(mk("vector"))
        block.tensor(mk("tensor"))
        block.gpsimd(mk("gpsimd"))

    st.close()
    return nc


def _stage(inputs):
    import ml_dtypes
    x = np.asarray(inputs["L0_attn_x"], dtype=np.float32)
    W_in = np.ascontiguousarray(np.asarray(inputs["L0_attn_W_in"], dtype=np.float32))
    conv_w = np.asarray(inputs["L0_attn_conv_w"], dtype=np.float32).reshape(D, 4)
    conv_b = np.asarray(inputs["L0_attn_conv_b"], dtype=np.float32)
    W_xp = np.ascontiguousarray(np.asarray(inputs["L0_attn_W_xproj"], dtype=np.float32))
    log_A = np.asarray(inputs["L0_attn_log_A"], dtype=np.float32)
    D_param = np.asarray(inputs["L0_attn_D_param"], dtype=np.float32)
    W_out = np.ascontiguousarray(np.asarray(inputs["L0_attn_W_out"], dtype=np.float32))
    q = np.asarray(inputs["L0_ffn_qweights"], dtype=np.int32)
    scales = np.asarray(inputs["L0_ffn_scales"], dtype=np.float32)
    biases = np.asarray(inputs["L0_ffn_biases"], dtype=np.float32)
    la = np.asarray(inputs["L0_ffn_lora_a"], dtype=np.float32)
    lb = np.asarray(inputs["L0_ffn_lora_b"], dtype=np.float32)
    g = np.asarray(inputs["L0_ffn_ln_gamma"], dtype=np.float32)
    b = np.asarray(inputs["L0_ffn_ln_beta"], dtype=np.float32)

    cols = np.zeros((128, COLS_W), dtype=np.float32)
    cols[:, OF_CONV:OF_CONV + 32] = conv_w.reshape(DH, 128, 4).transpose(1, 0, 2).reshape(128, 32)
    cols[:, OF_CB:OF_CB + 8] = conv_b.reshape(DH, 128).T
    cols[:, OF_D:OF_D + 8] = D_param.reshape(DH, 128).T
    cols[:, OF_BIAS:OF_BIAS + 48] = biases.reshape(6, DH, 128).transpose(2, 0, 1).reshape(128, 48)
    cols[:, OF_G:OF_G + 24] = g.reshape(3, DH, 128).transpose(2, 0, 1).reshape(128, 24)
    cols[:, OF_B:OF_B + 24] = b.reshape(3, DH, 128).transpose(2, 0, 1).reshape(128, 24)
    cols[:, OF_ONE] = 1.0
    cols[:, OF_EPS] = 1e-5
    cols[:, OF_LOGA:OF_LOGA + 128] = log_A.reshape(DH, 128, NST).transpose(1, 0, 2).reshape(128, 128)

    sel = np.zeros((128, 16 * 128), dtype=np.float32)
    for n in range(NST):
        sel[n, n * 128:(n + 1) * 128] = 1.0
    sel = sel.astype(ml_dtypes.bfloat16)
    zf32 = np.zeros((128, 1024), dtype=np.float32)
    zbf = np.zeros((128, 1024), dtype=np.float32).astype(ml_dtypes.bfloat16)
    zone_np = np.zeros((128, 128), dtype=np.float32)
    zone_np[0, :] = 1.0
    onesb_np = np.full((128, 128), 1.0 / 1024.0, dtype=np.float32).astype(ml_dtypes.bfloat16)

    qT = np.ascontiguousarray(q.transpose(0, 2, 1))
    sT = np.ascontiguousarray(np.repeat(scales, 16, axis=2).transpose(0, 2, 1))
    laT = np.ascontiguousarray(la.transpose(0, 2, 1))
    lbT = np.ascontiguousarray(lb.transpose(0, 2, 1))

    shared = dict(w_in=W_in, w_xp=W_xp, w_out=W_out, cols=cols, selB=sel,
                  zf32=zf32, zbf=zbf, zone_in=zone_np, onesb_in=onesb_np,
                  qT=qT, sT=sT, laT=laT, lbT=lbT)
    in_maps = []
    for bidx in range(8):
        m = dict(shared)
        m["x_fm"] = np.ascontiguousarray(x[bidx].T)
        in_maps.append(m)
    return in_maps


def kernel(**inputs):
    in_maps = _stage(inputs)
    nc = build_program()
    res = run_bass_kernel_spmd(nc, in_maps, core_ids=list(range(8)))
    out = np.stack([np.asarray(r["out_fm"]).T for r in res.results], axis=0)
    return np.ascontiguousarray(out.astype(np.float32))
